# version 02
# baseline (speedup 1.0000x reference)
import numpy as np
import ml_dtypes
import concourse.bass as bass
import concourse.mybir as mybir
from concourse.bass_utils import run_bass_kernel_spmd
from contextlib import ExitStack

F32 = mybir.dt.float32
BF16 = mybir.dt.bfloat16
AF = mybir.ActivationFunctionType
ALU = mybir.AluOpType
AX = mybir.AxisListType.X

P = 128
S = 2048
D = 1024
H = 4096
E = 8
NDT = D // P
NJC = H // P
TC = 1024
NCH = S // TC
NTS = TC // P

ENGS = ("tensor", "vector", "scalar", "gpsimd", "sync")
SEMS = ("wd0", "wd1", "we0", "we1", "wm0", "wm1", "wm2", "xdma", "cdma", "mm", "act", "dve", "pool")
OWN = {"tensor": "mm", "vector": "dve", "scalar": "act", "gpsimd": "pool"}


class Sched:
    def __init__(self):
        self.c = {k: 0 for k in SEMS}
        self.ops = {k: [] for k in ENGS}

    def emit(self, eng, fn, waits=None, inc=None):
        own = OWN.get(eng)
        w = {k: v for k, v in (waits or {}).items() if v > 0}
        if eng == "tensor":
            w["mm"] = self.c["mm"]
        elif own is not None:
            w[own] = self.c[own]
            if inc is None:
                inc = (own, 1)
        self.ops[eng].append((w, inc, fn))
        if inc:
            self.c[inc[0]] += inc[1]
            return self.c[inc[0]]
        return None

    def barrier(self):
        snap = dict(self.c)
        for eng in ("tensor", "vector", "scalar", "gpsimd"):
            w = {s: v for s, v in snap.items() if s != OWN[eng]}
            self.emit(eng, lambda g: g.nop(), waits=w, inc=(OWN[eng], 1))
        snap2 = dict(self.c)
        for eng in ("tensor", "vector", "scalar", "gpsimd", "sync"):
            w = {s: snap2[s] for s in SEMS}
            self.emit(eng, lambda g: g.nop(), waits=w)


def build_program(debug=False):
    nc = bass.Bass("TRN2", target_bir_lowering=False)

    inp = nc.dram_tensor("inp", [S, D], F32, kind="ExternalInput").ap()
    rw_d = nc.dram_tensor("rw", [E, D], F32, kind="ExternalInput").ap()
    rb_d = nc.dram_tensor("rb", [1, E], F32, kind="ExternalInput").ap()
    whg_d = nc.dram_tensor("whg", [E, 2 * D, D], F32, kind="ExternalInput").ap()
    wout_d = nc.dram_tensor("wout", [E, D, D], F32, kind="ExternalInput").ap()
    w1_d = nc.dram_tensor("w1", [E, D, H], F32, kind="ExternalInput").ap()
    w2_d = nc.dram_tensor("w2", [E, H, D], F32, kind="ExternalInput").ap()
    nw8_d = nc.dram_tensor("nw8h", [P, NDT], F32, kind="ExternalInput").ap()
    wr64_d = nc.dram_tensor("wr64h", [P, NDT * E], F32, kind="ExternalInput").ap()
    b1a_d = nc.dram_tensor("b1ah", [P, E * NJC], F32, kind="ExternalInput").ap()
    b2a_d = nc.dram_tensor("b2ah", [P, E * NDT], F32, kind="ExternalInput").ap()
    ident_d = nc.dram_tensor("ident", [P, P], F32, kind="ExternalInput").ap()
    esel_d = nc.dram_tensor("esel", [E, E * P], F32, kind="ExternalInput").ap()
    onesr_d = nc.dram_tensor("onesr", [1, P], F32, kind="ExternalInput").ap()
    onescf_d = nc.dram_tensor("onescf", [P, 1], F32, kind="ExternalInput").ap()
    cvals_d = nc.dram_tensor("cvals", [1, 2], F32, kind="ExternalInput").ap()
    out_d = nc.dram_tensor("out", [S, D], F32, kind="ExternalOutput").ap()
    xk = "ExternalOutput" if debug else "Internal"
    xf = [nc.dram_tensor(f"xf{i}", [D, S], F32, kind=xk).ap() for i in range(4)]

    K = Sched()

    with ExitStack() as top:
        def sb(name, shape, dt):
            return top.enter_context(nc.sbuf_tensor(name, shape, dt))

        sems = {s: top.enter_context(nc.semaphore(s)) for s in SEMS}
        ps = [top.enter_context(nc.psum_tensor(f"ps{i}", [P, 1024], F32)) for i in range(4)]

        ident = sb("identS", [P, P], F32)
        esel = sb("eselS", [E, E * P], F32)
        onesr = sb("onesrS", [1, P], F32)
        onescf = sb("onescfS", [P, 1], F32)
        onescb = sb("onescbS", [P, 1], BF16)
        nw8 = sb("nw8", [P, NDT], F32)
        rwraw = sb("rwraw", [E, D], F32)
        rwT = sb("rwT", [P, NDT * E], F32)
        rbrow = sb("rbrow", [1, E], F32)
        wr64 = sb("wr64", [P, NDT * E], F32)
        b1all = sb("b1all", [P, E * NJC], F32)
        b2all = sb("b2all", [P, E * NDT], F32)
        wsumb = sb("wsumb", [P, E], F32)
        probrow = sb("probrow", [1, E], F32)
        pcols = sb("pcols", [P, E], F32)
        r8a = sb("r8a", [1, E], F32)
        r8b = sb("r8b", [1, E], F32)
        tiny = sb("tinyS", [E, 1], F32)
        cvals = sb("cvalsS", [1, 2], F32)
        nc.const_aps.register(F32, 1e-5, cvals[0:1, 0:1])
        nc.const_aps.register(F32, 1e-6, cvals[0:1, 1:2])

        for (dst, src) in [(ident[:, :], ident_d[:, :]), (esel[:, :], esel_d[:, :]),
                           (onesr[:, :], onesr_d[:, :]), (onescf[:, :], onescf_d[:, :]),
                           (cvals[:, :], cvals_d[:, :]),
                           (rwraw[:, :], rw_d[:, :]),
                           (rbrow[:, :], rb_d[:, :]), (nw8[:, :], nw8_d[:, :]),
                           (wr64[:, :], wr64_d[:, :]), (b1all[:, :], b1a_d[:, :]),
                           (b2all[:, :], b2a_d[:, :])]:
            K.emit("gpsimd", (lambda d, s: lambda g: g.dma_start(out=d, in_=s))(dst, src),
                   inc=("cdma", 16))
        cload = K.c["cdma"]
        K.emit("vector", lambda g: g.tensor_copy(onescb[:, :], onescf[:, :]),
               waits={"cdma": cload}, inc=("dve", 1))

        for dt_i in range(NDT):
            mv = K.emit("tensor", (lambda i: lambda g: g.transpose(
                out=ps[i % 2][0:P, 0:8], in_=rwraw[0:E, i * P:(i + 1) * P],
                identity=ident[0:E, 0:E]))(dt_i),
                waits={"cdma": cload, "dve": K.c["dve"]}, inc=("mm", 1))
            K.emit("vector", (lambda i: lambda g: g.tensor_copy(
                rwT[:, i * 8:(i + 1) * 8], ps[i % 2][0:P, 0:8]))(dt_i),
                waits={"mm": mv}, inc=("dve", 1))
        for dt_i in range(NDT):
            mv = K.emit("tensor", (lambda i: lambda g: g.matmul(
                out=ps[2][0:E, 0:1], lhsT=wr64[:, i * 8:(i + 1) * 8], rhs=onescf[:, :],
                start=(i == 0), stop=(i == NDT - 1)))(dt_i),
                waits={"cdma": cload}, inc=("mm", 1) if dt_i == NDT - 1 else None)
        dv = K.emit("vector", lambda g: g.tensor_copy(tiny[:, :], ps[2][0:E, 0:1]),
                    waits={"mm": mv}, inc=("dve", 1))
        mv = K.emit("tensor", lambda g: g.transpose(out=ps[3][0:1, 0:E], in_=tiny[:, :],
                                                    identity=ident[0:E, 0:E]),
                    waits={"dve": dv}, inc=("mm", 1))
        dv = K.emit("vector", lambda g: g.tensor_copy(r8a[:, :], ps[3][0:1, 0:E]),
                    waits={"mm": mv}, inc=("dve", 1))
        dv = K.emit("vector", lambda g: g.tensor_scalar(
            out=r8a[:, :], in0=r8a[:, :], scalar1=-1.0, scalar2=None, op0=ALU.mult),
            inc=("dve", 1))
        mv = K.emit("tensor", lambda g: g.matmul(out=ps[3][0:P, 0:E], lhsT=onesr[:, :],
                                                 rhs=r8a[:, :], start=True, stop=True),
                    waits={"dve": dv}, inc=("mm", 1))
        K.emit("vector", lambda g: g.tensor_copy(wsumb[:, :], ps[3][0:P, 0:E]),
               waits={"mm": mv}, inc=("dve", 1))
        K.barrier()

        with ExitStack() as tsc:
            xtm = tsc.enter_context(nc.sbuf_tensor("xtm", [P, 16 * D], F32))
            ftile = tsc.enter_context(nc.sbuf_tensor("ftile", [P, 16 * D], F32))
            for st in range(S // P):
                K.emit("scalar", (lambda s_: lambda g: g.dma_start(
                    out=xtm[:, s_ * D:(s_ + 1) * D], in_=inp[s_ * P:(s_ + 1) * P, :]))(st),
                    inc=("xdma", 16))
            ld_all = K.c["xdma"]
            cp_ms = [0, 0]
            for st in range(S // P):
                sl = st % 2
                mm_last = 0
                for dt_i in range(NDT):
                    mm_last = K.emit("tensor", (lambda s_, q, i: lambda g: g.transpose(
                        out=ps[q][:, i * P:(i + 1) * P],
                        in_=xtm[:, s_ * D + i * P:s_ * D + (i + 1) * P],
                        identity=ident[:, :]))(st, sl, dt_i),
                        waits={"xdma": ld_all, "dve": cp_ms[sl]}, inc=("mm", 1))
                cp_ms[sl] = K.emit("vector", (lambda s_, q: lambda g: g.tensor_copy(
                    ftile[:, s_ * D:(s_ + 1) * D], ps[q][:, :]))(st, sl),
                    waits={"mm": mm_last}, inc=("dve", 1))
            cp_all = K.c["dve"]
            for st in range(S // P):
                K.emit("scalar", (lambda s_: lambda g: g.dma_start(
                    out=xf[0].rearrange("(dt p) s -> p dt s", p=P)[:, :, s_ * P:(s_ + 1) * P],
                    in_=ftile[:, s_ * D:(s_ + 1) * D].rearrange("p (dt s) -> p dt s", s=P)))(st),
                    waits={"dve": cp_all}, inc=("xdma", 16))
            tin_store = K.c["xdma"]
        K.barrier()

        def ffn_call(src_xf, dst_xf, src_ready, tag):
            with ExitStack() as fs:
                def fsb(name, shape, dt):
                    return fs.enter_context(nc.sbuf_tensor(name + tag, shape, dt))
                xc = fsb("xc", [P, NDT * TC], F32)
                u1 = fsb("u1", [P, NDT * TC], BF16)
                hsb = fsb("hsb", [P, NJC * TC], BF16)
                w1st = fsb("w1st", [P, 2 * 2048], F32)
                w1b = fsb("w1b", [P, 2 * 2048], BF16)
                w2st = fsb("w2st", [P, 2 * 1024], F32)
                w2b = fsb("w2b", [P, 2 * 1024], BF16)
                wgb = fsb("wgb", [P, 2 * TC], F32)
                tmpu = fsb("tmpu", [P, 1024], F32)
                tmpev = fsb("tmpev", [P, 1024], F32)
                bfscr = fsb("bfscr", [P, 4 * 1024], BF16)
                raw_tm = fsb("raw_tm", [P, NTS * E], F32)
                lg_tm = fsb("lg_tm", [P, NTS * E], F32)
                t1_tm = fsb("t1_tm", [P, NTS * E], F32)
                eq_tm = fsb("eq_tm", [P, NTS * E], F32)
                msk_tm = fsb("msk_tm", [P, NTS * E], F32)
                ge_tm = fsb("ge_tm", [P, NTS * E], F32)
                ex_tm = fsb("ex_tm", [P, NTS * E], F32)
                wg_tm = fsb("wg_tm", [P, NTS * E], F32)
                c8a = fsb("c8a", [P, NTS], F32)
                c8b = fsb("c8b", [P, NTS], F32)
                c8c = fsb("c8c", [P, NTS], F32)
                c8d = fsb("c8d", [P, NTS], F32)
                c8e = fsb("c8e", [P, NTS], F32)
                m_tm = fsb("m_tm", [P, NTS], F32)
                nr_tm = fsb("nr_tm", [P, NTS], F32)
                wgT = fsb("wgT", [E, TC], F32)
                rowm = fsb("rowm", [1, TC], F32)
                rowr = fsb("rowr", [1, TC], F32)
                rows1 = fsb("rows1", [1, TC], F32)
                rows2 = fsb("rows2", [1, TC], F32)

                w1st_rel = [0, 0]; w1b_rel = [0, 0]
                w2st_rel = [0, 0]; w2b_rel = [0, 0]
                wdsem = ["wd0", "wd1"]; wesem = ["we0", "we1"]
                ps_rel_dve = [0, 0, 0, 0]
                store_ms = src_ready

                for ch in range(NCH):
                    t0 = ch * TC
                    xld = 0
                    for dt_i in range(NDT):
                        xld = K.emit("scalar", (lambda i, tt: lambda g: g.dma_start(
                            out=xc[:, i * TC:(i + 1) * TC],
                            in_=src_xf[i * P:(i + 1) * P, tt:tt + TC]))(dt_i, t0),
                            waits={"xdma": store_ms}, inc=("xdma", 16))
                    sq_ms = [0, 0]
                    for dt_i in range(NDT):
                        sl = dt_i % 2
                        a1 = K.emit("scalar", (lambda i, q: lambda g: g.activation(
                            out=bfscr[:, q * 1024:(q + 1) * 1024], in_=xc[:, i * TC:(i + 1) * TC],
                            func=AF.Copy))(dt_i, sl),
                            waits={"xdma": xld, "mm": sq_ms[sl]}, inc=("act", 1))
                        a2 = K.emit("scalar", (lambda i, q: lambda g: g.activation(
                            out=bfscr[:, (2 + q) * 1024:(3 + q) * 1024], in_=xc[:, i * TC:(i + 1) * TC],
                            func=AF.Square))(dt_i, sl), inc=("act", 1))
                        mmv = 0
                        for sh in range(2):
                            K.emit("tensor", (lambda q, s_, i: lambda g: g.matmul(
                                out=ps[2][0:1, s_ * 512:(s_ + 1) * 512], lhsT=onescb[:, :],
                                rhs=bfscr[:, q * 1024 + s_ * 512:q * 1024 + (s_ + 1) * 512],
                                start=(i == 0), stop=(i == NDT - 1)))(sl, sh, dt_i),
                                waits={"act": a1, "dve": ps_rel_dve[2]})
                            mmv = K.emit("tensor", (lambda q, s_, i: lambda g: g.matmul(
                                out=ps[3][0:1, s_ * 512:(s_ + 1) * 512], lhsT=onescb[:, :],
                                rhs=bfscr[:, (2 + q) * 1024 + s_ * 512:(2 + q) * 1024 + (s_ + 1) * 512],
                                start=(i == 0), stop=(i == NDT - 1)))(sl, sh, dt_i),
                                waits={"act": a2, "dve": ps_rel_dve[3]}, inc=("mm", 1))
                        sq_ms[sl] = mmv
                    m_sums = K.c["mm"]
                    dv = K.emit("vector", lambda g: g.tensor_scalar(
                        out=rowm[:, :], in0=ps[2][0:1, 0:TC], scalar1=1.0 / D, scalar2=None,
                        op0=ALU.mult), waits={"mm": m_sums}, inc=("dve", 1))
                    av = K.emit("scalar", lambda g: g.activation(out=rows1[:, :], in_=rowm[:, :],
                                                                func=AF.Square),
                                waits={"dve": dv}, inc=("act", 1))
                    dv = K.emit("vector", lambda g: g.scalar_tensor_tensor(
                        out=rows2[:, :], in0=ps[3][0:1, 0:TC], scalar=1.0 / D, in1=rows1[:, :],
                        op0=ALU.mult, op1=ALU.subtract),
                        waits={"act": av}, inc=("dve", 1))
                    av = K.emit("scalar", lambda g: g.activation(out=rows1[:, :], in_=rows2[:, :],
                                                                func=AF.Sqrt, bias=1e-5),
                                waits={"dve": dv}, inc=("act", 1))
                    dvnr = K.emit("vector", lambda g: g.reciprocal(rowr[:, :], rows1[:, :]),
                                waits={"act": av}, inc=("dve", 1))
                    ps_rel_dve[2] = dvnr; ps_rel_dve[3] = dvnr
                    for k2 in range(2 * NTS):
                        ts_i = k2 % NTS
                        isnr = k2 >= NTS
                        pj = k2 % 4
                        row_src = rowr if isnr else rowm
                        mmv = K.emit("tensor", (lambda r_, t, j: lambda g: g.transpose(
                            out=ps[j][0:P, 32:33], in_=r_[0:1, t * P:(t + 1) * P],
                            identity=ident[0:1, 0:1]))(row_src, ts_i, pj),
                            waits={"dve": max(dvnr, ps_rel_dve[pj])}, inc=("mm", 1))
                        dst = nr_tm if isnr else m_tm
                        dvv = K.emit("vector", (lambda d2, t, j: lambda g: g.tensor_copy(
                            d2[:, t:t + 1], ps[j][0:P, 32:33]))(dst, ts_i, pj),
                            waits={"mm": mmv}, inc=("dve", 1))
                        ps_rel_dve[pj] = dvv
                    m_bc = 0
                    for sh in range(2):
                        K.emit("tensor", (lambda s_: lambda g: g.matmul(
                            out=ps[0][:, s_ * 512:(s_ + 1) * 512], lhsT=onesr[:, :],
                            rhs=rowm[0:1, s_ * 512:(s_ + 1) * 512], start=True, stop=True))(sh),
                            waits={"dve": ps_rel_dve[0]})
                        m_bc = K.emit("tensor", (lambda s_: lambda g: g.matmul(
                            out=ps[1][:, s_ * 512:(s_ + 1) * 512], lhsT=onesr[:, :],
                            rhs=rowr[0:1, s_ * 512:(s_ + 1) * 512], start=True, stop=True))(sh),
                            waits={"dve": ps_rel_dve[1]}, inc=("mm", 1))
                    u1_ms = 0
                    for dt_i in range(NDT):
                        K.emit("vector", (lambda i: lambda g: g.tensor_tensor(
                            out=tmpu[:, :], in0=xc[:, i * TC:(i + 1) * TC], in1=ps[0][:, 0:TC],
                            op=ALU.subtract))(dt_i),
                            waits={"mm": m_bc}, inc=("dve", 1))
                        u1_ms = K.emit("vector", (lambda i: lambda g: g.tensor_tensor(
                            out=u1[:, i * TC:(i + 1) * TC], in0=tmpu[:, :], in1=ps[1][:, 0:TC],
                            op=ALU.mult))(dt_i), inc=("dve", 1))
                    ps_rel_dve[0] = u1_ms; ps_rel_dve[1] = u1_ms
                    for ts_i in range(NTS):
                        pj = ts_i % 4
                        mmv = 0
                        for dt_i in range(NDT):
                            mmv = K.emit("tensor", (lambda t, i, j: lambda g: g.matmul(
                                out=ps[j][:, 0:8], lhsT=xc[:, i * TC + t * P:i * TC + (t + 1) * P],
                                rhs=wr64[:, i * 8:(i + 1) * 8],
                                start=(i == 0), stop=(i == NDT - 1)))(ts_i, dt_i, pj),
                                waits={"dve": ps_rel_dve[pj]},
                                inc=("mm", 1) if dt_i == NDT - 1 else None)
                        dvv = K.emit("vector", (lambda t, j: lambda g: g.tensor_copy(
                            raw_tm[:, t * 8:(t + 1) * 8], ps[j][:, 0:8]))(ts_i, pj),
                            waits={"mm": mmv}, inc=("dve", 1))
                        ps_rel_dve[pj] = dvv
                    def stage(fn_maker, eng="vector", waits=None, incsem="dve"):
                        last = 0
                        for t in range(NTS):
                            last = K.emit(eng, fn_maker(t), waits=waits, inc=(incsem, 1))
                        return last
                    stage(lambda t: lambda g: g.scalar_tensor_tensor(
                        out=t1_tm[:, t * 8:(t + 1) * 8], in0=wsumb[:, :], scalar=m_tm[:, t:t + 1],
                        in1=raw_tm[:, t * 8:(t + 1) * 8], op0=ALU.mult, op1=ALU.add))
                    stage(lambda t: lambda g: g.tensor_scalar(
                        out=lg_tm[:, t * 8:(t + 1) * 8], in0=t1_tm[:, t * 8:(t + 1) * 8],
                        scalar1=nr_tm[:, t:t + 1], scalar2=None, op0=ALU.mult))
                    stage(lambda t: lambda g: g.reduce_max(
                        c8a[:, t:t + 1], lg_tm[:, t * 8:(t + 1) * 8], axis=AX))
                    stage(lambda t: lambda g: g.tensor_scalar(
                        out=c8b[:, t:t + 1], in0=c8a[:, t:t + 1], scalar1=-1.0, scalar2=None,
                        op0=ALU.mult))
                    stage(lambda t: lambda g: g.tensor_scalar(
                        out=eq_tm[:, t * 8:(t + 1) * 8], in0=lg_tm[:, t * 8:(t + 1) * 8],
                        scalar1=c8a[:, t:t + 1], scalar2=None, op0=ALU.is_equal))
                    stage(lambda t: lambda g: g.scalar_tensor_tensor(
                        out=msk_tm[:, t * 8:(t + 1) * 8], in0=eq_tm[:, t * 8:(t + 1) * 8],
                        scalar=-1e30, in1=lg_tm[:, t * 8:(t + 1) * 8], op0=ALU.mult, op1=ALU.add))
                    stage(lambda t: lambda g: g.reduce_max(
                        c8c[:, t:t + 1], msk_tm[:, t * 8:(t + 1) * 8], axis=AX))
                    stage(lambda t: lambda g: g.tensor_scalar(
                        out=ge_tm[:, t * 8:(t + 1) * 8], in0=lg_tm[:, t * 8:(t + 1) * 8],
                        scalar1=c8c[:, t:t + 1], scalar2=None, op0=ALU.is_ge))
                    d_lg = K.c["dve"]
                    a_ex = stage(lambda t: lambda g: g.activation(
                        out=ex_tm[:, t * 8:(t + 1) * 8], in_=lg_tm[:, t * 8:(t + 1) * 8],
                        func=AF.Exp, bias=c8b[:, t:t + 1]), eng="scalar",
                        waits={"dve": d_lg}, incsem="act")
                    stage(lambda t: lambda g: g.reduce_sum(
                        c8d[:, t:t + 1], ex_tm[:, t * 8:(t + 1) * 8], axis=AX),
                        waits={"act": a_ex})
                    stage(lambda t: lambda g: g.reciprocal(c8e[:, t:t + 1], c8d[:, t:t + 1]))
                    stage(lambda t: lambda g: g.tensor_scalar(
                        out=wg_tm[:, t * 8:(t + 1) * 8], in0=ex_tm[:, t * 8:(t + 1) * 8],
                        scalar1=c8e[:, t:t + 1], scalar2=0.5, op0=ALU.mult, op1=ALU.mult))
                    d_wg = stage(lambda t: lambda g: g.tensor_tensor(
                        out=wg_tm[:, t * 8:(t + 1) * 8], in0=wg_tm[:, t * 8:(t + 1) * 8],
                        in1=ge_tm[:, t * 8:(t + 1) * 8], op=ALU.mult))
                    for ts_i in range(NTS):
                        pj = ts_i % 4
                        mmv = K.emit("tensor", (lambda t, j: lambda g: g.transpose(
                            out=ps[j][0:8, 0:P], in_=wg_tm[:, t * 8:(t + 1) * 8],
                            identity=ident[:, :]))(ts_i, pj),
                            waits={"dve": max(d_wg, ps_rel_dve[pj])}, inc=("mm", 1))
                        dvv = K.emit("vector", (lambda t, j: lambda g: g.tensor_copy(
                            wgT[0:E, t * P:(t + 1) * P], ps[j][0:8, 0:P]))(ts_i, pj),
                            waits={"mm": mmv}, inc=("dve", 1))
                        ps_rel_dve[pj] = dvv
                    wgT_done = K.c["dve"]

                    gelu_all_prev = 0
                    for e in range(E):
                        wsl = e % 2
                        mmv = 0
                        for sh in range(2):
                            mmv = K.emit("tensor", (lambda ee, s_: lambda g: g.matmul(
                                out=ps[2][:, s_ * 512:(s_ + 1) * 512],
                                lhsT=esel[0:E, ee * P:(ee + 1) * P],
                                rhs=wgT[0:E, s_ * 512:(s_ + 1) * 512], start=True, stop=True))(e, sh),
                                waits={"dve": max(ps_rel_dve[2], wgT_done)},
                                inc=("mm", 1) if sh == 1 else None)
                        d_wgb = K.emit("vector", (lambda q: lambda g: g.tensor_copy(
                            wgb[:, q * TC:(q + 1) * TC], ps[2][:, 0:TC]))(wsl),
                            waits={"mm": mmv}, inc=("dve", 1))
                        ps_rel_dve[2] = d_wgb
                        gelu_ms = {}
                        w1cast = 0
                        for jc in range(NJC):
                            if jc % 2 == 0:
                                pr = (jc // 2) % 2
                                wld = K.emit("sync", (lambda ee, j, q: lambda g: g.dma_start(
                                    out=w1st[:, q * 2048:(q + 1) * 2048].rearrange(
                                        "p (dt j) -> p dt j", j=2 * P),
                                    in_=w1_d[ee].rearrange("(dt p) h -> p dt h", p=P)[:, :, j * P:(j + 2) * P]))(e, jc, pr),
                                    waits={"dve": w1st_rel[pr]}, inc=(wdsem[pr], 16))
                                w1cast = K.emit("vector", (lambda q: lambda g: g.tensor_copy(
                                    w1b[:, q * 2048:(q + 1) * 2048], w1st[:, q * 2048:(q + 1) * 2048]))(pr),
                                    waits={wdsem[pr]: wld, "mm": w1b_rel[pr]}, inc=("dve", 1))
                                w1st_rel[pr] = w1cast
                            pr = (jc // 2) % 2
                            joff = jc % 2
                            hpr = jc % 2
                            mmv = 0
                            for sh in range(2):
                                for dt_i in range(NDT):
                                    w = {"dve": w1cast}
                                    if sh == 0 and dt_i == 0:
                                        w["act"] = gelu_ms.get(jc - 2, gelu_all_prev)
                                        w["dve"] = max(w1cast, ps_rel_dve[hpr])
                                    mmv = K.emit("tensor", (lambda q, jo, i, s_, hp: lambda g: g.matmul(
                                        out=ps[hp][:, s_ * 512:(s_ + 1) * 512],
                                        lhsT=w1b[:, q * 2048 + i * 256 + jo * P:
                                                 q * 2048 + i * 256 + jo * P + P],
                                        rhs=u1[:, i * TC + s_ * 512:i * TC + (s_ + 1) * 512],
                                        start=(i == 0), stop=(i == NDT - 1)))(pr, joff, dt_i, sh, hpr),
                                        waits=w,
                                        inc=("mm", 1) if (sh == 1 and dt_i == NDT - 1) else None)
                            w1b_rel[pr] = mmv
                            ps_rel_dve[hpr] = max(ps_rel_dve[hpr], 0)
                            gelu_ms[jc] = K.emit("scalar", (lambda ee, j, hp: lambda g: g.activation(
                                out=hsb[:, j * TC:(j + 1) * TC], in_=ps[hp][:, 0:TC],
                                func=AF.Gelu_apprx_tanh,
                                bias=b1all[:, ee * NJC + j:ee * NJC + j + 1]))(e, jc, hpr),
                                waits={"mm": mmv}, inc=("act", 1))
                        gelu_all = gelu_ms[NJC - 1]
                        gelu_all_prev = gelu_all
                        w2cast = 0
                        for half in range(2):
                            for jc in range(NJC):
                                if jc % 2 == 0:
                                    pr = (jc // 2) % 2
                                    wld = K.emit("sync", (lambda ee, j, hf, q: lambda g: g.dma_start(
                                        out=w2st[:, q * 1024:(q + 1) * 1024].rearrange(
                                            "p (jj o) -> p jj o", o=512),
                                        in_=w2_d[ee].rearrange("(jj p) o -> p jj o", p=P)[:, j:j + 2, hf * 512:(hf + 1) * 512]))(e, jc, half, pr),
                                        waits={"act": w2st_rel[pr]}, inc=(wesem[pr], 16))
                                    w2cast = K.emit("scalar", (lambda q: lambda g: g.activation(
                                        out=w2b[:, q * 1024:(q + 1) * 1024],
                                        in_=w2st[:, q * 1024:(q + 1) * 1024], func=AF.Copy))(pr),
                                        waits={wesem[pr]: wld, "mm": w2b_rel[pr]}, inc=("act", 1))
                                    w2st_rel[pr] = w2cast
                                pr = (jc // 2) % 2
                                joff = jc % 2
                                mmv = 0
                                for sh in range(2):
                                    for oc_rel in range(4):
                                        w = {"act": max(w2cast, gelu_ms[jc])}
                                        if jc == 0 and sh == 0:
                                            w["act"] = max(w["act"], gelu_all)
                                            w["dve"] = ps_rel_dve[oc_rel]
                                        mmv = K.emit("tensor", (lambda j, q, jo, s_, o: lambda g: g.matmul(
                                            out=ps[o][:, s_ * 512:(s_ + 1) * 512],
                                            lhsT=w2b[:, q * 1024 + jo * 512 + o * P:
                                                     q * 1024 + jo * 512 + o * P + P],
                                            rhs=hsb[:, j * TC + s_ * 512:j * TC + (s_ + 1) * 512],
                                            start=(j == 0), stop=(j == NJC - 1)))(jc, pr, joff, sh, oc_rel),
                                            waits=w,
                                            inc=("mm", 1) if (sh == 1 and oc_rel == 3) else None)
                                w2b_rel[pr] = mmv
                            m_p2 = mmv
                            for oc_rel in range(4):
                                oc = half * 4 + oc_rel
                                dvv = K.emit("vector", (lambda ee, o, orl, q: lambda g: g.scalar_tensor_tensor(
                                    out=tmpev[:, :], in0=ps[orl][:, 0:TC],
                                    scalar=b2all[:, ee * NDT + o:ee * NDT + o + 1],
                                    in1=wgb[:, q * TC:(q + 1) * TC],
                                    op0=ALU.add, op1=ALU.mult))(e, oc, oc_rel, wsl),
                                    waits={"mm": m_p2}, inc=("dve", 1))
                                ps_rel_dve[oc_rel] = dvv
                                K.emit("vector", (lambda o: lambda g: g.tensor_tensor(
                                    out=xc[:, o * TC:(o + 1) * TC], in0=xc[:, o * TC:(o + 1) * TC],
                                    in1=tmpev[:, :], op=ALU.add))(oc), inc=("dve", 1))
                    ev_end = K.c["dve"]
                    for dt_i in range(NDT):
                        store_ms = K.emit("scalar", (lambda i, tt: lambda g: g.dma_start(
                            out=dst_xf[i * P:(i + 1) * P, tt:tt + TC],
                            in_=xc[:, i * TC:(i + 1) * TC]))(dt_i, t0),
                            waits={"dve": ev_end}, inc=("xdma", 16))
            K.barrier()
            return store_ms

        ffn1_store = ffn_call(xf[0], xf[1], tin_store, "A")

        with ExitStack() as ms:
            def msb(name, shape, dt):
                return ms.enter_context(nc.sbuf_tensor(name, shape, dt))
            mx = msb("mx", [P, NDT * S], F32)
            xn = msb("xn", [P, NDT * S], BF16)
            xnsA = msb("xnsA", [P, NDT], F32)
            xnsB = msb("xnsB", [P, NDT], F32)
            xns = msb("xns", [P, NDT], F32)
            ps_rel_dve = [0, 0, 0, 0]
            ps_rel_act = [0, 0, 0, 0]

            xld = 0
            for dt_i in range(NDT):
                xld = K.emit("scalar", (lambda i: lambda g: g.dma_start(
                    out=mx[:, i * S:(i + 1) * S], in_=xf[1][i * P:(i + 1) * P, :]))(dt_i),
                    waits={"xdma": ffn1_store}, inc=("xdma", 16))
            ts2 = ExitStack()
            t_t = ts2.enter_context(nc.sbuf_tensor("t_t", [P, NDT * S], F32))
            mrowm = ts2.enter_context(nc.sbuf_tensor("mrowm", [1, S], F32))
            mrowr = ts2.enter_context(nc.sbuf_tensor("mrowr", [1, S], F32))
            mrows1 = ts2.enter_context(nc.sbuf_tensor("mrows1", [1, S], F32))
            bfscr2 = ts2.enter_context(nc.sbuf_tensor("bfscr2", [P, 2 * S], BF16))
            sq_ms = [0, 0]
            for dt_i in range(NDT):
                sl = dt_i % 2
                a1 = K.emit("scalar", (lambda i, q: lambda g: g.activation(
                    out=bfscr2[:, q * S:(q + 1) * S], in_=mx[:, i * S:(i + 1) * S],
                    func=AF.Copy))(dt_i, sl),
                    waits={"xdma": xld, "mm": sq_ms[sl]}, inc=("act", 1))
                mmv = 0
                for sc in range(4):
                    pj = 2 + sc // 2
                    mmv = K.emit("tensor", (lambda q, s_, j, i: lambda g: g.matmul(
                        out=ps[j][0:1, (s_ % 2) * 512:(s_ % 2 + 1) * 512], lhsT=onescb[:, :],
                        rhs=bfscr2[:, q * S + s_ * 512:q * S + (s_ + 1) * 512],
                        start=(i == 0), stop=(i == NDT - 1)))(sl, sc, pj, dt_i),
                        waits={"act": a1}, inc=("mm", 1) if sc == 3 else None)
                a2 = K.emit("scalar", (lambda i, q: lambda g: g.activation(
                    out=bfscr2[:, q * S:(q + 1) * S], in_=mx[:, i * S:(i + 1) * S],
                    func=AF.Square))(dt_i, sl),
                    waits={"mm": mmv}, inc=("act", 1))
                for sc in range(4):
                    pj = sc // 2
                    mmv = K.emit("tensor", (lambda q, s_, j, i: lambda g: g.matmul(
                        out=ps[j][0:1, (s_ % 2) * 512:(s_ % 2 + 1) * 512], lhsT=onescb[:, :],
                        rhs=bfscr2[:, q * S + s_ * 512:q * S + (s_ + 1) * 512],
                        start=(i == 0), stop=(i == NDT - 1)))(sl, sc, pj, dt_i),
                        waits={"act": a2}, inc=("mm", 1) if sc == 3 else None)
                sq_ms[sl] = mmv
            m_sums = K.c["mm"]
            dv = 0
            for hf in range(2):
                dv = K.emit("vector", (lambda h2: lambda g: g.tensor_scalar(
                    out=mrowm[0:1, h2 * 1024:(h2 + 1) * 1024], in0=ps[2 + h2][0:1, 0:1024],
                    scalar1=1.0 / D, scalar2=None, op0=ALU.mult))(hf),
                    waits={"mm": m_sums}, inc=("dve", 1))
            av = K.emit("scalar", lambda g: g.activation(out=mrows1[:, :], in_=mrowm[:, :],
                                                        func=AF.Square),
                        waits={"dve": dv}, inc=("act", 1))
            for hf in range(2):
                dv = K.emit("vector", (lambda h2: lambda g: g.scalar_tensor_tensor(
                    out=mrows1[0:1, h2 * 1024:(h2 + 1) * 1024], in0=ps[h2][0:1, 0:1024],
                    scalar=1.0 / D, in1=mrows1[0:1, h2 * 1024:(h2 + 1) * 1024],
                    op0=ALU.mult, op1=ALU.subtract))(hf),
                    waits={"act": av}, inc=("dve", 1))
            av = K.emit("scalar", lambda g: g.activation(out=mrowr[:, :], in_=mrows1[:, :],
                                                        func=AF.Sqrt, bias=1e-5),
                        waits={"dve": dv}, inc=("act", 1))
            d_rows = K.emit("vector", lambda g: g.reciprocal(mrows1[:, :], mrowr[:, :]),
                            waits={"act": av}, inc=("dve", 1))
            m_bc = 0
            for sc in range(4):
                K.emit("tensor", (lambda s_: lambda g: g.matmul(
                    out=ps[s_ // 2][:, (s_ % 2) * 512:(s_ % 2 + 1) * 512], lhsT=onesr[:, :],
                    rhs=mrowm[0:1, s_ * 512:(s_ + 1) * 512], start=True, stop=True))(sc),
                    waits={"dve": d_rows})
                m_bc = K.emit("tensor", (lambda s_: lambda g: g.matmul(
                    out=ps[2 + s_ // 2][:, (s_ % 2) * 512:(s_ % 2 + 1) * 512], lhsT=onesr[:, :],
                    rhs=mrows1[0:1, s_ * 512:(s_ + 1) * 512], start=True, stop=True))(sc),
                    waits={"dve": d_rows}, inc=("mm", 1))
            t_ms = 0
            for dt_i in range(NDT):
                for hf in range(2):
                    K.emit("vector", (lambda i, h2: lambda g: g.tensor_tensor(
                        out=t_t[:, i * S + h2 * 1024:i * S + (h2 + 1) * 1024],
                        in0=mx[:, i * S + h2 * 1024:i * S + (h2 + 1) * 1024],
                        in1=ps[h2][:, 0:1024], op=ALU.subtract))(dt_i, hf),
                        waits={"mm": m_bc}, inc=("dve", 1))
                    t_ms = K.emit("vector", (lambda i, h2: lambda g: g.tensor_tensor(
                        out=t_t[:, i * S + h2 * 1024:i * S + (h2 + 1) * 1024],
                        in0=t_t[:, i * S + h2 * 1024:i * S + (h2 + 1) * 1024],
                        in1=ps[2 + h2][:, 0:1024], op=ALU.mult))(dt_i, hf), inc=("dve", 1))
            sq_ms = [0, 0]
            mmv = 0
            for dt_i in range(NDT):
                sl = dt_i % 2
                a1 = K.emit("scalar", (lambda i, q: lambda g: g.activation(
                    out=bfscr2[:, q * S:(q + 1) * S], in_=t_t[:, i * S:(i + 1) * S],
                    func=AF.Square))(dt_i, sl),
                    waits={"dve": t_ms, "mm": sq_ms[sl]}, inc=("act", 1))
                for sc in range(4):
                    pj = sc // 2
                    w = {"act": a1}
                    if dt_i == 0 and sc == 0:
                        w["dve"] = t_ms
                    mmv = K.emit("tensor", (lambda q, s_, j, i: lambda g: g.matmul(
                        out=ps[j][0:1, (s_ % 2) * 512:(s_ % 2 + 1) * 512], lhsT=onescb[:, :],
                        rhs=bfscr2[:, q * S + s_ * 512:q * S + (s_ + 1) * 512],
                        start=(i == 0), stop=(i == NDT - 1)))(sl, sc, pj, dt_i),
                        waits=w, inc=("mm", 1) if sc == 3 else None)
                sq_ms[sl] = mmv
            m_tsq = K.c["mm"]
            dv = 0
            for hf in range(2):
                dv = K.emit("vector", (lambda h2: lambda g: g.tensor_scalar(
                    out=mrowr[0:1, h2 * 1024:(h2 + 1) * 1024], in0=ps[h2][0:1, 0:1024],
                    scalar1=1.0 / D, scalar2=None, op0=ALU.mult))(hf),
                    waits={"mm": m_tsq}, inc=("dve", 1))
            av = K.emit("scalar", lambda g: g.activation(out=mrowm[:, :], in_=mrowr[:, :],
                                                        func=AF.Sqrt, bias=1e-6),
                        waits={"dve": dv}, inc=("act", 1))
            dv = K.emit("vector", lambda g: g.reciprocal(mrowr[:, :], mrowm[:, :]),
                        waits={"act": av}, inc=("dve", 1))
            m_rb = 0
            for sc in range(4):
                m_rb = K.emit("tensor", (lambda s_: lambda g: g.matmul(
                    out=ps[s_ // 2][:, (s_ % 2) * 512:(s_ % 2 + 1) * 512], lhsT=onesr[:, :],
                    rhs=mrowr[0:1, s_ * 512:(s_ + 1) * 512], start=True, stop=True))(sc),
                    waits={"dve": dv}, inc=("mm", 1))
            xn_ms = 0
            for dt_i in range(NDT):
                for hf in range(2):
                    acc = xnsA if hf == 0 else xnsB
                    xn_ms = K.emit("vector", (lambda i, h2, a3: lambda g: g.scalar_tensor_tensor(
                        out=xn[:, i * S + h2 * 1024:i * S + (h2 + 1) * 1024],
                        in0=t_t[:, i * S + h2 * 1024:i * S + (h2 + 1) * 1024],
                        scalar=nw8[:, i:i + 1], in1=ps[h2][:, 0:1024],
                        op0=ALU.mult, op1=ALU.mult, accum_out=a3[:, i:i + 1]))(dt_i, hf, acc),
                        waits={"mm": m_rb}, inc=("dve", 1))
            d_xnsum = K.emit("vector", lambda g: g.tensor_tensor(
                out=xns[:, :], in0=xnsA[:, :], in1=xnsB[:, :], op=ALU.add), inc=("dve", 1))
            ps_rel_dve = [xn_ms, xn_ms, t_ms, t_ms]
            for dt_i in range(NDT):
                K.emit("vector", (lambda i: lambda g: g.tensor_tensor(
                    out=mx[:, i * S:(i + 1) * S], in0=mx[:, i * S:(i + 1) * S],
                    in1=t_t[:, i * S:(i + 1) * S], op=ALU.add))(dt_i), inc=("dve", 1))
            ts2.close()
            mmv = 0
            for dt_i in range(NDT):
                mmv = K.emit("tensor", (lambda i: lambda g: g.matmul(
                    out=ps[0][0:1, 0:8], lhsT=xns[:, i:i + 1], rhs=rwT[:, i * 8:(i + 1) * 8],
                    start=(i == 0), stop=(i == NDT - 1)))(dt_i),
                    waits={"dve": max(d_xnsum, ps_rel_dve[0])},
                    inc=("mm", 1) if dt_i == NDT - 1 else None)
            K.emit("vector", lambda g: g.tensor_scalar(
                out=r8a[:, :], in0=ps[0][0:1, 0:8], scalar1=1.0 / S, scalar2=None, op0=ALU.mult),
                waits={"mm": mmv}, inc=("dve", 1))
            K.emit("vector", lambda g: g.tensor_tensor(out=r8a[:, :], in0=r8a[:, :],
                                                       in1=rbrow[:, :], op=ALU.add), inc=("dve", 1))
            K.emit("vector", lambda g: g.reduce_max(r8b[0:1, 0:1], r8a[:, :], axis=AX),
                   inc=("dve", 1))
            dv = K.emit("vector", lambda g: g.tensor_scalar(
                out=r8b[0:1, 1:2], in0=r8b[0:1, 0:1], scalar1=-1.0, scalar2=None, op0=ALU.mult),
                inc=("dve", 1))
            av = K.emit("scalar", lambda g: g.activation(out=probrow[:, :], in_=r8a[:, :],
                                                        func=AF.Exp, bias=r8b[0:1, 1:2]),
                        waits={"dve": dv}, inc=("act", 1))
            K.emit("vector", lambda g: g.reduce_sum(r8b[0:1, 2:3], probrow[:, :], axis=AX),
                   waits={"act": av}, inc=("dve", 1))
            K.emit("vector", lambda g: g.reciprocal(r8b[0:1, 3:4], r8b[0:1, 2:3]), inc=("dve", 1))
            dv = K.emit("vector", lambda g: g.tensor_scalar(
                out=probrow[:, :], in0=probrow[:, :], scalar1=r8b[0:1, 3:4], scalar2=None,
                op0=ALU.mult), inc=("dve", 1))
            mmv = K.emit("tensor", lambda g: g.transpose(out=ps[1][0:8, 0:1], in_=probrow[:, :],
                                                        identity=ident[0:1, 0:1]),
                         waits={"dve": max(dv, ps_rel_dve[1])}, inc=("mm", 1))
            dvp = K.emit("vector", lambda g: g.tensor_copy(tiny[:, :], ps[1][0:8, 0:1]),
                         waits={"mm": mmv}, inc=("dve", 1))
            for e in range(E):
                mmv = K.emit("tensor", (lambda ee: lambda g: g.matmul(
                    out=ps[1][:, 8:9], lhsT=esel[0:E, ee * P:(ee + 1) * P], rhs=tiny[:, :],
                    start=True, stop=True))(e),
                    waits={"dve": K.c["dve"]}, inc=("mm", 1))
                K.emit("vector", (lambda ee: lambda g: g.tensor_copy(
                    pcols[:, ee:ee + 1], ps[1][:, 8:9]))(e),
                    waits={"mm": mmv}, inc=("dve", 1))
            d_pcols = K.c["dve"]

            with ExitStack() as hs2:
                hbf = hs2.enter_context(nc.sbuf_tensor("hbf", [P, NDT * S], BF16))
                hid_t = hs2.enter_context(nc.sbuf_tensor("hid_t", [P, S], F32))
                gat_t = hs2.enter_context(nc.sbuf_tensor("gat_t", [P, S], F32))
                ct_t = hs2.enter_context(nc.sbuf_tensor("ct_t", [P, S], F32))
                scr_t = hs2.enter_context(nc.sbuf_tensor("scr_t", [P, S], F32))
                macc = hs2.enter_context(nc.sbuf_tensor("macc", [P, 2 * 1024], F32))
                wtT = hs2.enter_context(nc.sbuf_tensor("wtT", [P, NDT * 256], BF16))
                wotT = hs2.enter_context(nc.sbuf_tensor("wotT", [P, NDT * P], BF16))
                wst = hs2.enter_context(nc.sbuf_tensor("wst", [P, 3 * 1024], F32))
                wst_rel = [d_pcols, d_pcols, d_pcols]
                wmsem = ["wm0", "wm1", "wm2"]
                slot_i = 0
                hg_done = 0
                for cq in range(NDT):
                    fma_h = 0; fma_g = 0
                    for e in range(E):
                        for part in range(2):
                            slq = slot_i % 3; slot_i += 1
                            roff = cq * P if part == 0 else D + cq * P
                            wld = K.emit("sync", (lambda ee, ro, q: lambda g: g.dma_start(
                                out=wst[:, q * 1024:(q + 1) * 1024],
                                in_=whg_d[ee, ro:ro + P, :]))(e, roff, slq),
                                waits={"dve": wst_rel[slq]}, inc=(wmsem[slq], 16))
                            tgt = 0 if part == 0 else 1024
                            if e == 0:
                                dvv = K.emit("vector", (lambda q, tg, ee: lambda g: g.tensor_scalar(
                                    out=macc[:, tg:tg + 1024], in0=wst[:, q * 1024:(q + 1) * 1024],
                                    scalar1=pcols[:, ee:ee + 1], scalar2=None, op0=ALU.mult))(slq, tgt, e),
                                    waits={wmsem[slq]: wld, "dve": d_pcols}, inc=("dve", 1))
                            else:
                                dvv = K.emit("vector", (lambda q, tg, ee: lambda g: g.scalar_tensor_tensor(
                                    out=macc[:, tg:tg + 1024], in0=wst[:, q * 1024:(q + 1) * 1024],
                                    scalar=pcols[:, ee:ee + 1], in1=macc[:, tg:tg + 1024],
                                    op0=ALU.mult, op1=ALU.add))(slq, tgt, e),
                                    waits={wmsem[slq]: wld}, inc=("dve", 1))
                            wst_rel[slq] = dvv
                            if part == 0:
                                fma_h = dvv
                            else:
                                fma_g = dvv
                    a_tp = 0
                    for part in range(2):
                        src_off = 0 if part == 0 else 1024
                        for dt_i in range(NDT):
                            pj = 2 + (dt_i % 2)
                            mmv = K.emit("tensor", (lambda so, i, j: lambda g: g.transpose(
                                out=ps[j][:, 128:256], in_=macc[:, so + i * P:so + (i + 1) * P],
                                identity=ident[:, :]))(src_off, dt_i, pj),
                                waits={"dve": max(fma_h, fma_g), "act": ps_rel_act[pj]},
                                inc=("mm", 1))
                            a_tp = K.emit("scalar", (lambda i, j, pt: lambda g: g.activation(
                                out=wtT[:, i * 256 + pt * P:i * 256 + (pt + 1) * P],
                                in_=ps[j][:, 128:256], func=AF.Copy))(dt_i, pj, part),
                                waits={"mm": mmv}, inc=("act", 1))
                            ps_rel_act[pj] = a_tp
                    mm_hid = 0; mm_gat = 0
                    for part in range(2):
                        for sc in range(4):
                            pj = (0 if part == 0 else 2) + sc // 2
                            mmv = 0
                            for dt_i in range(NDT):
                                w = {"act": max(a_tp, ps_rel_act[pj])} if dt_i == 0 else {"act": a_tp}
                                if dt_i == 0:
                                    w["dve"] = ps_rel_dve[pj]
                                mmv = K.emit("tensor", (lambda pt, s_, j, i: lambda g: g.matmul(
                                    out=ps[j][:, (s_ % 2) * 512:(s_ % 2 + 1) * 512],
                                    lhsT=wtT[:, i * 256 + pt * P:i * 256 + (pt + 1) * P],
                                    rhs=xn[:, i * S + s_ * 512:i * S + (s_ + 1) * 512],
                                    start=(i == 0), stop=(i == NDT - 1)))(part, sc, pj, dt_i),
                                    waits=w, inc=("mm", 1) if dt_i == NDT - 1 else None)
                            if part == 0:
                                mm_hid = mmv
                            else:
                                mm_gat = mmv
                            ps_rel_dve[pj] = 0
                    a_h = 0; a_g = 0
                    for part in range(2):
                        for hf in range(2):
                            pj = (0 if part == 0 else 2) + hf
                            dst = hid_t if part == 0 else gat_t
                            av2 = K.emit("scalar", (lambda j, d3, h2, pm: lambda g: g.activation(
                                out=d3[:, h2 * 1024:(h2 + 1) * 1024], in_=ps[j][:, 0:1024],
                                func=AF.Copy))(pj, dst, hf, part),
                                waits={"mm": (mm_hid if part == 0 else mm_gat)}, inc=("act", 1))
                            ps_rel_act[pj] = av2
                            if part == 0:
                                a_h = av2
                            else:
                                a_g = av2
                    a_c = K.emit("scalar", lambda g: g.activation(out=ct_t[:, :], in_=gat_t[:, :],
                                                                 func=AF.Sigmoid, scale=-1.0),
                                 waits={"act": a_g, "dve": hg_done}, inc=("act", 1))
                    a_z = K.emit("scalar", lambda g: g.activation(out=scr_t[:, :], in_=gat_t[:, :],
                                                                 func=AF.Sigmoid), inc=("act", 1))
                    a_s = K.emit("scalar", lambda g: g.activation(out=gat_t[:, :], in_=hid_t[:, :],
                                                                 func=AF.Sigmoid),
                                 waits={"act": a_h}, inc=("act", 1))
                    K.emit("vector", lambda g: g.tensor_scalar(out=hid_t[:, :], in0=hid_t[:, :],
                                                               scalar1=0.5, scalar2=None, op0=ALU.add),
                           waits={"act": a_s}, inc=("dve", 1))
                    K.emit("vector", lambda g: g.tensor_tensor(out=hid_t[:, :], in0=hid_t[:, :],
                                                               in1=gat_t[:, :], op=ALU.max),
                           inc=("dve", 1))
                    K.emit("vector", lambda g: g.tensor_tensor(out=hid_t[:, :], in0=hid_t[:, :],
                                                               in1=scr_t[:, :], op=ALU.mult),
                           waits={"act": a_z}, inc=("dve", 1))
                    K.emit("vector", lambda g: g.tensor_tensor_scan(
                        out=scr_t[:, :], data0=ct_t[:, :], data1=hid_t[:, :], initial=0.0,
                        op0=ALU.mult, op1=ALU.add),
                        waits={"act": a_c}, inc=("dve", 1))
                    hg_done = K.emit("vector", (lambda c2: lambda g: g.tensor_copy(
                        hbf[:, c2 * S:(c2 + 1) * S], scr_t[:, :]))(cq), inc=("dve", 1))
                for oq in range(NDT):
                    fma_o = 0
                    for e in range(E):
                        slq = slot_i % 3; slot_i += 1
                        wld = K.emit("sync", (lambda ee, ro, q: lambda g: g.dma_start(
                            out=wst[:, q * 1024:(q + 1) * 1024],
                            in_=wout_d[ee, ro:ro + P, :]))(e, oq * P, slq),
                            waits={"dve": wst_rel[slq]}, inc=(wmsem[slq], 16))
                        if e == 0:
                            fma_o = K.emit("vector", (lambda q, ee: lambda g: g.tensor_scalar(
                                out=macc[:, 0:1024], in0=wst[:, q * 1024:(q + 1) * 1024],
                                scalar1=pcols[:, ee:ee + 1], scalar2=None, op0=ALU.mult))(slq, e),
                                waits={wmsem[slq]: wld}, inc=("dve", 1))
                        else:
                            fma_o = K.emit("vector", (lambda q, ee: lambda g: g.scalar_tensor_tensor(
                                out=macc[:, 0:1024], in0=wst[:, q * 1024:(q + 1) * 1024],
                                scalar=pcols[:, ee:ee + 1], in1=macc[:, 0:1024],
                                op0=ALU.mult, op1=ALU.add))(slq, e),
                                waits={wmsem[slq]: wld}, inc=("dve", 1))
                        wst_rel[slq] = fma_o
                    a_tp = 0
                    for kt in range(NDT):
                        pj = 2 + (kt % 2)
                        mmv = K.emit("tensor", (lambda i, j: lambda g: g.transpose(
                            out=ps[j][:, 128:256], in_=macc[:, i * P:(i + 1) * P],
                            identity=ident[:, :]))(kt, pj),
                            waits={"dve": fma_o, "act": ps_rel_act[pj]}, inc=("mm", 1))
                        a_tp = K.emit("scalar", (lambda i, j: lambda g: g.activation(
                            out=wotT[:, i * P:(i + 1) * P], in_=ps[j][:, 128:256],
                            func=AF.Copy))(kt, pj),
                            waits={"mm": mmv}, inc=("act", 1))
                        ps_rel_act[pj] = a_tp
                    for hf in range(2):
                        pj = hf
                        mmv = 0
                        for sc in range(2):
                            for kt in range(NDT):
                                w = {"act": a_tp, "dve": hg_done}
                                if kt == 0 and sc == 0:
                                    w["dve"] = max(hg_done, ps_rel_dve[pj])
                                mmv = K.emit("tensor", (lambda h2, s_, j, i: lambda g: g.matmul(
                                    out=ps[j][:, s_ * 512:(s_ + 1) * 512],
                                    lhsT=wotT[:, i * P:(i + 1) * P],
                                    rhs=hbf[:, i * S + h2 * 1024 + s_ * 512:
                                            i * S + h2 * 1024 + (s_ + 1) * 512],
                                    start=(i == 0), stop=(i == NDT - 1)))(hf, sc, pj, kt),
                                    waits=w,
                                    inc=("mm", 1) if (kt == NDT - 1 and sc == 1) else None)
                        dvv = K.emit("vector", (lambda o, h2, j: lambda g: g.tensor_tensor(
                            out=mx[:, o * S + h2 * 1024:o * S + (h2 + 1) * 1024],
                            in0=mx[:, o * S + h2 * 1024:o * S + (h2 + 1) * 1024],
                            in1=ps[j][:, 0:1024], op=ALU.add))(oq, hf, pj),
                            waits={"mm": mmv}, inc=("dve", 1))
                        ps_rel_dve[pj] = dvv
            ev_end = K.c["dve"]
            mix_store = 0
            for dt_i in range(NDT):
                mix_store = K.emit("scalar", (lambda i: lambda g: g.dma_start(
                    out=xf[2][i * P:(i + 1) * P, :], in_=mx[:, i * S:(i + 1) * S]))(dt_i),
                    waits={"dve": ev_end}, inc=("xdma", 16))
        K.barrier()

        ffn2_store = ffn_call(xf[2], xf[3], mix_store, "B")

        with ExitStack() as tsc:
            xtm_o = tsc.enter_context(nc.sbuf_tensor("xtm2", [P, 16 * D], F32))
            ftile_o = tsc.enter_context(nc.sbuf_tensor("ftile2", [P, 16 * D], F32))
            for st in range(S // P):
                K.emit("scalar", (lambda s_: lambda g: g.dma_start(
                    out=xtm_o[:, s_ * D:(s_ + 1) * D].rearrange("p (dt s) -> p dt s", s=P),
                    in_=xf[3].rearrange("(dt p) s -> p dt s", p=P)[:, :, s_ * P:(s_ + 1) * P]))(st),
                    waits={"xdma": ffn2_store}, inc=("xdma", 16))
            ld_all = K.c["xdma"]
            cp_ms = [0, 0]
            for st in range(S // P):
                sl = st % 2
                mm_last = 0
                for dt_i in range(NDT):
                    mm_last = K.emit("tensor", (lambda s_, q, i: lambda g: g.transpose(
                        out=ps[q][:, i * P:(i + 1) * P],
                        in_=xtm_o[:, s_ * D + i * P:s_ * D + (i + 1) * P],
                        identity=ident[:, :]))(st, sl, dt_i),
                        waits={"xdma": ld_all, "dve": cp_ms[sl]}, inc=("mm", 1))
                cp_ms[sl] = K.emit("vector", (lambda s_, q: lambda g: g.tensor_copy(
                    ftile_o[:, s_ * D:(s_ + 1) * D], ps[q][:, :]))(st, sl),
                    waits={"mm": mm_last}, inc=("dve", 1))
            cp_all = K.c["dve"]
            for st in range(S // P):
                K.emit("scalar", (lambda s_: lambda g: g.dma_start(
                    out=out_d[s_ * P:(s_ + 1) * P, :], in_=ftile_o[:, s_ * D:(s_ + 1) * D]))(st),
                    waits={"dve": cp_all}, inc=("xdma", 16))
        K.emit("sync", lambda g: g.nop(), waits={"xdma": K.c["xdma"]})

        with nc.Block() as block:
            def replay(name):
                def _(eng):
                    waited = {}
                    for w, inc, fn in K.ops[name]:
                        for s2, v in w.items():
                            if v > waited.get(s2, 0):
                                eng.wait_ge(sems[s2], v)
                                waited[s2] = v
                        r = fn(eng)
                        if inc:
                            r.then_inc(sems[inc[0]], inc[1])
                return _
            block.tensor(replay("tensor"))
            block.vector(replay("vector"))
            block.scalar(replay("scalar"))
            block.gpsimd(replay("gpsimd"))
            block.sync(replay("sync"))

    return nc


_AUX = None


def _aux():
    global _AUX
    if _AUX is None:
        esel = np.zeros((E, E * P), dtype=np.float32)
        for e in range(E):
            esel[e, e * P:(e + 1) * P] = 1.0
        _AUX = {
            "ident": np.eye(P, dtype=np.float32),
            "esel": esel,
            "onesr": np.ones((1, P), dtype=np.float32),
            "onescf": np.ones((P, 1), dtype=np.float32),
            "cvals": np.array([[1e-5, 1e-6]], dtype=np.float32),
        }
    return _AUX


_NC_CACHE = None


def kernel(L0_attn_inputs, L0_attn_norm_weight, L0_attn_router_w, L0_attn_router_b,
           L0_attn_w_hg, L0_attn_w_out, L0_ffn_w_router, L0_ffn_w1, L0_ffn_b1,
           L0_ffn_w2, L0_ffn_b2):
    global _NC_CACHE
    if _NC_CACHE is None:
        _NC_CACHE = build_program(debug=False)
    nc = _NC_CACHE
    aux = _aux()
    f32 = lambda a: np.ascontiguousarray(a, dtype=np.float32)
    nw = f32(L0_attn_norm_weight)
    wr = f32(L0_ffn_w_router)
    b1 = f32(L0_ffn_b1)
    b2 = f32(L0_ffn_b2)
    common = {
        "rw": f32(L0_attn_router_w),
        "rb": f32(L0_attn_router_b).reshape(1, E),
        "whg": f32(L0_attn_w_hg),
        "wout": f32(L0_attn_w_out),
        "w1": f32(L0_ffn_w1),
        "w2": f32(L0_ffn_w2),
        "nw8h": np.ascontiguousarray(nw.reshape(NDT, P).T),
        "wr64h": np.ascontiguousarray(wr.reshape(NDT, P, E).transpose(1, 0, 2).reshape(P, NDT * E)),
        "b1ah": np.ascontiguousarray(b1.reshape(E, NJC, P).transpose(2, 0, 1).reshape(P, E * NJC)),
        "b2ah": np.ascontiguousarray(b2.reshape(E, NDT, P).transpose(2, 0, 1).reshape(P, E * NDT)),
        **aux,
    }
    x = f32(L0_attn_inputs)
    in_maps = [dict(common, inp=np.ascontiguousarray(x[b])) for b in range(8)]
    res = run_bass_kernel_spmd(nc, in_maps, core_ids=list(range(8)))
    return np.stack([r["out"] for r in res.results], axis=0)


# version 03
# speedup vs baseline: 1.0945x; 1.0945x over previous
import numpy as np
import ml_dtypes
import concourse.bass as bass
import concourse.mybir as mybir
from concourse.bass_utils import run_bass_kernel_spmd
from contextlib import ExitStack

F32 = mybir.dt.float32
BF16 = mybir.dt.bfloat16
AF = mybir.ActivationFunctionType
ALU = mybir.AluOpType
AX = mybir.AxisListType.X

P = 128
S = 2048
D = 1024
H = 4096
E = 8
NDT = D // P
NJC = H // P
TC = 1024
NCH = S // TC
NTS = TC // P

SIM_MODE = False

ENGS = ("tensor", "vector", "scalar", "gpsimd", "sync")
SEMS = ("wd0", "wd1", "we0", "we1", "wm0", "wm1", "wm2", "wm3", "wm4", "wm5", "xdma", "cdma", "mm", "act", "dve", "pool")
OWN = {"tensor": "mm", "vector": "dve", "scalar": "act", "gpsimd": "pool"}


class Sched:
    def __init__(self):
        self.c = {k: 0 for k in SEMS}
        self.ops = {k: [] for k in ENGS}

    def emit(self, eng, fn, waits=None, inc=None):
        own = OWN.get(eng)
        w = {k: v for k, v in (waits or {}).items() if v > 0}
        if eng == "tensor":
            if SIM_MODE:
                w["mm"] = self.c["mm"]
            else:
                w.pop("mm", None)
        elif own is not None:
            w[own] = self.c[own]
            if inc is None:
                inc = (own, 1)
        self.ops[eng].append((w, inc, fn))
        if inc:
            self.c[inc[0]] += inc[1]
            return self.c[inc[0]]
        return None

    def barrier(self):
        snap = dict(self.c)
        for eng in ("tensor", "vector", "scalar", "gpsimd"):
            w = {s: v for s, v in snap.items() if s != OWN[eng]}
            self.emit(eng, lambda g: g.nop(), waits=w, inc=(OWN[eng], 1))
        snap2 = dict(self.c)
        for eng in ("tensor", "vector", "scalar", "gpsimd", "sync"):
            w = {s: snap2[s] for s in SEMS}
            self.emit(eng, lambda g: g.nop(), waits=w)


def build_program(debug=False):
    nc = bass.Bass("TRN2", target_bir_lowering=False)

    inp = nc.dram_tensor("inp", [S, D], F32, kind="ExternalInput").ap()
    rw_d = nc.dram_tensor("rw", [E, D], F32, kind="ExternalInput").ap()
    rb_d = nc.dram_tensor("rb", [1, E], F32, kind="ExternalInput").ap()
    whg_d = nc.dram_tensor("whg", [E, 2 * D, D], F32, kind="ExternalInput").ap()
    wout_d = nc.dram_tensor("wout", [E, D, D], F32, kind="ExternalInput").ap()
    w1_d = nc.dram_tensor("w1", [E, D, H], F32, kind="ExternalInput").ap()
    w2_d = nc.dram_tensor("w2", [E, H, D], F32, kind="ExternalInput").ap()
    nw8_d = nc.dram_tensor("nw8h", [P, NDT], F32, kind="ExternalInput").ap()
    wr64_d = nc.dram_tensor("wr64h", [P, NDT * E], F32, kind="ExternalInput").ap()
    b1a_d = nc.dram_tensor("b1ah", [P, E * NJC], F32, kind="ExternalInput").ap()
    b2a_d = nc.dram_tensor("b2ah", [P, E * NDT], F32, kind="ExternalInput").ap()
    ident_d = nc.dram_tensor("ident", [P, P], F32, kind="ExternalInput").ap()
    esel_d = nc.dram_tensor("esel", [E, E * P], F32, kind="ExternalInput").ap()
    onesr_d = nc.dram_tensor("onesr", [1, P], F32, kind="ExternalInput").ap()
    onescf_d = nc.dram_tensor("onescf", [P, 1], F32, kind="ExternalInput").ap()
    cvals_d = nc.dram_tensor("cvals", [1, 2], F32, kind="ExternalInput").ap()
    out_d = nc.dram_tensor("out", [S, D], F32, kind="ExternalOutput").ap()
    xk = "ExternalOutput" if debug else "Internal"
    xf = [nc.dram_tensor(f"xf{i}", [D, S], F32, kind=xk).ap() for i in range(4)]

    K = Sched()

    with ExitStack() as top:
        def sb(name, shape, dt):
            return top.enter_context(nc.sbuf_tensor(name, shape, dt))

        sems = {s: top.enter_context(nc.semaphore(s)) for s in SEMS}
        ps = [top.enter_context(nc.psum_tensor(f"ps{i}", [P, 1024], F32)) for i in range(4)]

        ident = sb("identS", [P, P], F32)
        esel = sb("eselS", [E, E * P], F32)
        onesr = sb("onesrS", [1, P], F32)
        onescf = sb("onescfS", [P, 1], F32)
        onescb = sb("onescbS", [P, 1], BF16)
        nw8 = sb("nw8", [P, NDT], F32)
        rwraw = sb("rwraw", [E, D], F32)
        rwT = sb("rwT", [P, NDT * E], F32)
        rbrow = sb("rbrow", [1, E], F32)
        wr64 = sb("wr64", [P, NDT * E], F32)
        b1all = sb("b1all", [P, E * NJC], F32)
        b2all = sb("b2all", [P, E * NDT], F32)
        wsumb = sb("wsumb", [P, E], F32)
        probrow = sb("probrow", [1, E], F32)
        pcols = sb("pcols", [P, E], F32)
        r8a = sb("r8a", [1, E], F32)
        r8b = sb("r8b", [1, E], F32)
        tiny = sb("tinyS", [E, 1], F32)
        cvals = sb("cvalsS", [1, 2], F32)
        nc.const_aps.register(F32, 1e-5, cvals[0:1, 0:1])
        nc.const_aps.register(F32, 1e-6, cvals[0:1, 1:2])

        for (dst, src) in [(ident[:, :], ident_d[:, :]), (esel[:, :], esel_d[:, :]),
                           (onesr[:, :], onesr_d[:, :]), (onescf[:, :], onescf_d[:, :]),
                           (cvals[:, :], cvals_d[:, :]),
                           (rwraw[:, :], rw_d[:, :]),
                           (rbrow[:, :], rb_d[:, :]), (nw8[:, :], nw8_d[:, :]),
                           (wr64[:, :], wr64_d[:, :]), (b1all[:, :], b1a_d[:, :]),
                           (b2all[:, :], b2a_d[:, :])]:
            K.emit("gpsimd", (lambda d, s: lambda g: g.dma_start(out=d, in_=s))(dst, src),
                   inc=("cdma", 16))
        cload = K.c["cdma"]
        K.emit("vector", lambda g: g.tensor_copy(onescb[:, :], onescf[:, :]),
               waits={"cdma": cload}, inc=("dve", 1))

        for dt_i in range(NDT):
            mv = K.emit("tensor", (lambda i: lambda g: g.transpose(
                out=ps[i % 2][0:P, 0:8], in_=rwraw[0:E, i * P:(i + 1) * P],
                identity=ident[0:E, 0:E]))(dt_i),
                waits={"cdma": cload, "dve": K.c["dve"]}, inc=("mm", 1))
            K.emit("vector", (lambda i: lambda g: g.tensor_copy(
                rwT[:, i * 8:(i + 1) * 8], ps[i % 2][0:P, 0:8]))(dt_i),
                waits={"mm": mv}, inc=("dve", 1))
        for dt_i in range(NDT):
            mv = K.emit("tensor", (lambda i: lambda g: g.matmul(
                out=ps[2][0:E, 0:1], lhsT=wr64[:, i * 8:(i + 1) * 8], rhs=onescf[:, :],
                start=(i == 0), stop=(i == NDT - 1)))(dt_i),
                waits={"cdma": cload}, inc=("mm", 1) if dt_i == NDT - 1 else None)
        dv = K.emit("vector", lambda g: g.tensor_copy(tiny[:, :], ps[2][0:E, 0:1]),
                    waits={"mm": mv}, inc=("dve", 1))
        mv = K.emit("tensor", lambda g: g.transpose(out=ps[3][0:1, 0:E], in_=tiny[:, :],
                                                    identity=ident[0:E, 0:E]),
                    waits={"dve": dv}, inc=("mm", 1))
        dv = K.emit("vector", lambda g: g.tensor_copy(r8a[:, :], ps[3][0:1, 0:E]),
                    waits={"mm": mv}, inc=("dve", 1))
        dv = K.emit("vector", lambda g: g.tensor_scalar(
            out=r8a[:, :], in0=r8a[:, :], scalar1=-1.0, scalar2=None, op0=ALU.mult),
            inc=("dve", 1))
        mv = K.emit("tensor", lambda g: g.matmul(out=ps[3][0:P, 0:E], lhsT=onesr[:, :],
                                                 rhs=r8a[:, :], start=True, stop=True),
                    waits={"dve": dv}, inc=("mm", 1))
        K.emit("vector", lambda g: g.tensor_copy(wsumb[:, :], ps[3][0:P, 0:E]),
               waits={"mm": mv}, inc=("dve", 1))
        K.barrier()

        with ExitStack() as tsc:
            xtm = tsc.enter_context(nc.sbuf_tensor("xtm", [P, 16 * D], F32))
            ftile = tsc.enter_context(nc.sbuf_tensor("ftile", [P, 16 * D], F32))
            for st in range(S // P):
                K.emit("scalar", (lambda s_: lambda g: g.dma_start(
                    out=xtm[:, s_ * D:(s_ + 1) * D], in_=inp[s_ * P:(s_ + 1) * P, :]))(st),
                    inc=("xdma", 16))
            ld_all = K.c["xdma"]
            cp_ms = [0, 0]
            for st in range(S // P):
                sl = st % 2
                mm_last = 0
                for dt_i in range(NDT):
                    mm_last = K.emit("tensor", (lambda s_, q, i: lambda g: g.transpose(
                        out=ps[q][:, i * P:(i + 1) * P],
                        in_=xtm[:, s_ * D + i * P:s_ * D + (i + 1) * P],
                        identity=ident[:, :]))(st, sl, dt_i),
                        waits={"xdma": ld_all, "dve": cp_ms[sl]}, inc=("mm", 1))
                cp_ms[sl] = K.emit("vector", (lambda s_, q: lambda g: g.tensor_copy(
                    ftile[:, s_ * D:(s_ + 1) * D], ps[q][:, :]))(st, sl),
                    waits={"mm": mm_last}, inc=("dve", 1))
            cp_all = K.c["dve"]
            for st in range(S // P):
                K.emit("scalar", (lambda s_: lambda g: g.dma_start(
                    out=xf[0].rearrange("(dt p) s -> p dt s", p=P)[:, :, s_ * P:(s_ + 1) * P],
                    in_=ftile[:, s_ * D:(s_ + 1) * D].rearrange("p (dt s) -> p dt s", s=P)))(st),
                    waits={"dve": cp_all}, inc=("xdma", 16))
            tin_store = K.c["xdma"]
        K.barrier()

        def ffn_call(src_xf, dst_xf, src_ready, tag):
            with ExitStack() as fs:
                def fsb(name, shape, dt):
                    return fs.enter_context(nc.sbuf_tensor(name + tag, shape, dt))
                xc = fsb("xc", [P, NDT * TC], F32)
                u1 = fsb("u1", [P, NDT * TC], BF16)
                hsb = fsb("hsb", [P, NJC * TC], BF16)
                w1st = fsb("w1st", [P, 2 * 2048], F32)
                w1b = fsb("w1b", [P, 2 * 2048], BF16)
                w2st = fsb("w2st", [P, 2 * 1024], F32)
                w2b = fsb("w2b", [P, 2 * 1024], BF16)
                wgb = fsb("wgb", [P, 2 * TC], F32)
                tmpu = fsb("tmpu", [P, 1024], F32)
                tmpev = fsb("tmpev", [P, 1024], F32)
                bfscr = fsb("bfscr", [P, 4 * 1024], BF16)
                raw_tm = fsb("raw_tm", [P, NTS * E], F32)
                lg_tm = fsb("lg_tm", [P, NTS * E], F32)
                t1_tm = fsb("t1_tm", [P, NTS * E], F32)
                eq_tm = fsb("eq_tm", [P, NTS * E], F32)
                msk_tm = fsb("msk_tm", [P, NTS * E], F32)
                ge_tm = fsb("ge_tm", [P, NTS * E], F32)
                ex_tm = fsb("ex_tm", [P, NTS * E], F32)
                wg_tm = fsb("wg_tm", [P, NTS * E], F32)
                c8a = fsb("c8a", [P, NTS], F32)
                c8b = fsb("c8b", [P, NTS], F32)
                c8c = fsb("c8c", [P, NTS], F32)
                c8d = fsb("c8d", [P, NTS], F32)
                c8e = fsb("c8e", [P, NTS], F32)
                m_tm = fsb("m_tm", [P, NTS], F32)
                nr_tm = fsb("nr_tm", [P, NTS], F32)
                wgT = fsb("wgT", [E, TC], F32)
                rowm = fsb("rowm", [1, TC], F32)
                rowr = fsb("rowr", [1, TC], F32)
                rows1 = fsb("rows1", [1, TC], F32)
                rows2 = fsb("rows2", [1, TC], F32)

                w1st_rel = [0, 0]; w1b_rel = [0, 0]
                w2st_rel = [0, 0]; w2b_rel = [0, 0]
                wdsem = ["wd0", "wd1"]; wesem = ["we0", "we1"]
                ps_rel_dve = [0, 0, 0, 0]
                store_ms = src_ready

                for ch in range(NCH):
                    t0 = ch * TC
                    xld = 0
                    for dt_i in range(NDT):
                        xld = K.emit("scalar", (lambda i, tt: lambda g: g.dma_start(
                            out=xc[:, i * TC:(i + 1) * TC],
                            in_=src_xf[i * P:(i + 1) * P, tt:tt + TC]))(dt_i, t0),
                            waits={"xdma": store_ms}, inc=("xdma", 16))
                    sq_ms = [0, 0]
                    for dt_i in range(NDT):
                        sl = dt_i % 2
                        a1 = K.emit("scalar", (lambda i, q: lambda g: g.activation(
                            out=bfscr[:, q * 1024:(q + 1) * 1024], in_=xc[:, i * TC:(i + 1) * TC],
                            func=AF.Copy))(dt_i, sl),
                            waits={"xdma": xld, "mm": sq_ms[sl]}, inc=("act", 1))
                        a2 = K.emit("scalar", (lambda i, q: lambda g: g.activation(
                            out=bfscr[:, (2 + q) * 1024:(3 + q) * 1024], in_=xc[:, i * TC:(i + 1) * TC],
                            func=AF.Square))(dt_i, sl), inc=("act", 1))
                        mmv = 0
                        for sh in range(2):
                            K.emit("tensor", (lambda q, s_, i: lambda g: g.matmul(
                                out=ps[2][0:1, s_ * 512:(s_ + 1) * 512], lhsT=onescb[:, :],
                                rhs=bfscr[:, q * 1024 + s_ * 512:q * 1024 + (s_ + 1) * 512],
                                start=(i == 0), stop=(i == NDT - 1)))(sl, sh, dt_i),
                                waits={"act": a1, "dve": ps_rel_dve[2]})
                            mmv = K.emit("tensor", (lambda q, s_, i: lambda g: g.matmul(
                                out=ps[3][0:1, s_ * 512:(s_ + 1) * 512], lhsT=onescb[:, :],
                                rhs=bfscr[:, (2 + q) * 1024 + s_ * 512:(2 + q) * 1024 + (s_ + 1) * 512],
                                start=(i == 0), stop=(i == NDT - 1)))(sl, sh, dt_i),
                                waits={"act": a2, "dve": ps_rel_dve[3]}, inc=("mm", 1))
                        sq_ms[sl] = mmv
                    m_sums = K.c["mm"]
                    dv = K.emit("vector", lambda g: g.tensor_scalar(
                        out=rowm[:, :], in0=ps[2][0:1, 0:TC], scalar1=1.0 / D, scalar2=None,
                        op0=ALU.mult), waits={"mm": m_sums}, inc=("dve", 1))
                    av = K.emit("scalar", lambda g: g.activation(out=rows1[:, :], in_=rowm[:, :],
                                                                func=AF.Square),
                                waits={"dve": dv}, inc=("act", 1))
                    dv = K.emit("vector", lambda g: g.scalar_tensor_tensor(
                        out=rows2[:, :], in0=ps[3][0:1, 0:TC], scalar=1.0 / D, in1=rows1[:, :],
                        op0=ALU.mult, op1=ALU.subtract),
                        waits={"act": av}, inc=("dve", 1))
                    av = K.emit("scalar", lambda g: g.activation(out=rows1[:, :], in_=rows2[:, :],
                                                                func=AF.Sqrt, bias=1e-5),
                                waits={"dve": dv}, inc=("act", 1))
                    dvnr = K.emit("vector", lambda g: g.reciprocal(rowr[:, :], rows1[:, :]),
                                waits={"act": av}, inc=("dve", 1))
                    ps_rel_dve[2] = dvnr; ps_rel_dve[3] = dvnr
                    for k2 in range(2 * NTS):
                        ts_i = k2 % NTS
                        isnr = k2 >= NTS
                        pj = k2 % 4
                        row_src = rowr if isnr else rowm
                        mmv = K.emit("tensor", (lambda r_, t, j: lambda g: g.transpose(
                            out=ps[j][0:P, 32:33], in_=r_[0:1, t * P:(t + 1) * P],
                            identity=ident[0:1, 0:1]))(row_src, ts_i, pj),
                            waits={"dve": max(dvnr, ps_rel_dve[pj])}, inc=("mm", 1))
                        dst = nr_tm if isnr else m_tm
                        dvv = K.emit("vector", (lambda d2, t, j: lambda g: g.tensor_copy(
                            d2[:, t:t + 1], ps[j][0:P, 32:33]))(dst, ts_i, pj),
                            waits={"mm": mmv}, inc=("dve", 1))
                        ps_rel_dve[pj] = dvv
                    m_bc = 0
                    for sh in range(2):
                        K.emit("tensor", (lambda s_: lambda g: g.matmul(
                            out=ps[0][:, s_ * 512:(s_ + 1) * 512], lhsT=onesr[:, :],
                            rhs=rowm[0:1, s_ * 512:(s_ + 1) * 512], start=True, stop=True))(sh),
                            waits={"dve": ps_rel_dve[0]})
                        m_bc = K.emit("tensor", (lambda s_: lambda g: g.matmul(
                            out=ps[1][:, s_ * 512:(s_ + 1) * 512], lhsT=onesr[:, :],
                            rhs=rowr[0:1, s_ * 512:(s_ + 1) * 512], start=True, stop=True))(sh),
                            waits={"dve": ps_rel_dve[1]}, inc=("mm", 1))
                    u1_ms = 0
                    for dt_i in range(NDT):
                        K.emit("vector", (lambda i: lambda g: g.tensor_tensor(
                            out=tmpu[:, :], in0=xc[:, i * TC:(i + 1) * TC], in1=ps[0][:, 0:TC],
                            op=ALU.subtract))(dt_i),
                            waits={"mm": m_bc}, inc=("dve", 1))
                        u1_ms = K.emit("vector", (lambda i: lambda g: g.tensor_tensor(
                            out=u1[:, i * TC:(i + 1) * TC], in0=tmpu[:, :], in1=ps[1][:, 0:TC],
                            op=ALU.mult))(dt_i), inc=("dve", 1))
                    ps_rel_dve[0] = u1_ms; ps_rel_dve[1] = u1_ms
                    for ts_i in range(NTS):
                        pj = ts_i % 4
                        mmv = 0
                        for dt_i in range(NDT):
                            mmv = K.emit("tensor", (lambda t, i, j: lambda g: g.matmul(
                                out=ps[j][:, 0:8], lhsT=xc[:, i * TC + t * P:i * TC + (t + 1) * P],
                                rhs=wr64[:, i * 8:(i + 1) * 8],
                                start=(i == 0), stop=(i == NDT - 1)))(ts_i, dt_i, pj),
                                waits={"dve": ps_rel_dve[pj]},
                                inc=("mm", 1) if dt_i == NDT - 1 else None)
                        dvv = K.emit("vector", (lambda t, j: lambda g: g.tensor_copy(
                            raw_tm[:, t * 8:(t + 1) * 8], ps[j][:, 0:8]))(ts_i, pj),
                            waits={"mm": mmv}, inc=("dve", 1))
                        ps_rel_dve[pj] = dvv
                    def stage(fn_maker, eng="vector", waits=None, incsem="dve"):
                        last = 0
                        for t in range(NTS):
                            last = K.emit(eng, fn_maker(t), waits=waits, inc=(incsem, 1))
                        return last
                    stage(lambda t: lambda g: g.scalar_tensor_tensor(
                        out=t1_tm[:, t * 8:(t + 1) * 8], in0=wsumb[:, :], scalar=m_tm[:, t:t + 1],
                        in1=raw_tm[:, t * 8:(t + 1) * 8], op0=ALU.mult, op1=ALU.add))
                    stage(lambda t: lambda g: g.tensor_scalar(
                        out=lg_tm[:, t * 8:(t + 1) * 8], in0=t1_tm[:, t * 8:(t + 1) * 8],
                        scalar1=nr_tm[:, t:t + 1], scalar2=None, op0=ALU.mult))
                    stage(lambda t: lambda g: g.reduce_max(
                        c8a[:, t:t + 1], lg_tm[:, t * 8:(t + 1) * 8], axis=AX))
                    stage(lambda t: lambda g: g.tensor_scalar(
                        out=c8b[:, t:t + 1], in0=c8a[:, t:t + 1], scalar1=-1.0, scalar2=None,
                        op0=ALU.mult))
                    stage(lambda t: lambda g: g.tensor_scalar(
                        out=eq_tm[:, t * 8:(t + 1) * 8], in0=lg_tm[:, t * 8:(t + 1) * 8],
                        scalar1=c8a[:, t:t + 1], scalar2=None, op0=ALU.is_equal))
                    stage(lambda t: lambda g: g.scalar_tensor_tensor(
                        out=msk_tm[:, t * 8:(t + 1) * 8], in0=eq_tm[:, t * 8:(t + 1) * 8],
                        scalar=-1e30, in1=lg_tm[:, t * 8:(t + 1) * 8], op0=ALU.mult, op1=ALU.add))
                    stage(lambda t: lambda g: g.reduce_max(
                        c8c[:, t:t + 1], msk_tm[:, t * 8:(t + 1) * 8], axis=AX))
                    stage(lambda t: lambda g: g.tensor_scalar(
                        out=ge_tm[:, t * 8:(t + 1) * 8], in0=lg_tm[:, t * 8:(t + 1) * 8],
                        scalar1=c8c[:, t:t + 1], scalar2=None, op0=ALU.is_ge))
                    d_lg = K.c["dve"]
                    a_ex = stage(lambda t: lambda g: g.activation(
                        out=ex_tm[:, t * 8:(t + 1) * 8], in_=lg_tm[:, t * 8:(t + 1) * 8],
                        func=AF.Exp, bias=c8b[:, t:t + 1]), eng="scalar",
                        waits={"dve": d_lg}, incsem="act")
                    stage(lambda t: lambda g: g.reduce_sum(
                        c8d[:, t:t + 1], ex_tm[:, t * 8:(t + 1) * 8], axis=AX),
                        waits={"act": a_ex})
                    stage(lambda t: lambda g: g.reciprocal(c8e[:, t:t + 1], c8d[:, t:t + 1]))
                    stage(lambda t: lambda g: g.tensor_scalar(
                        out=wg_tm[:, t * 8:(t + 1) * 8], in0=ex_tm[:, t * 8:(t + 1) * 8],
                        scalar1=c8e[:, t:t + 1], scalar2=0.5, op0=ALU.mult, op1=ALU.mult))
                    d_wg = stage(lambda t: lambda g: g.tensor_tensor(
                        out=wg_tm[:, t * 8:(t + 1) * 8], in0=wg_tm[:, t * 8:(t + 1) * 8],
                        in1=ge_tm[:, t * 8:(t + 1) * 8], op=ALU.mult))
                    for ts_i in range(NTS):
                        pj = ts_i % 4
                        mmv = K.emit("tensor", (lambda t, j: lambda g: g.transpose(
                            out=ps[j][0:8, 0:P], in_=wg_tm[:, t * 8:(t + 1) * 8],
                            identity=ident[:, :]))(ts_i, pj),
                            waits={"dve": max(d_wg, ps_rel_dve[pj])}, inc=("mm", 1))
                        dvv = K.emit("vector", (lambda t, j: lambda g: g.tensor_copy(
                            wgT[0:E, t * P:(t + 1) * P], ps[j][0:8, 0:P]))(ts_i, pj),
                            waits={"mm": mmv}, inc=("dve", 1))
                        ps_rel_dve[pj] = dvv
                    wgT_done = K.c["dve"]

                    gelu_all_prev = 0
                    for e in range(E):
                        wsl = e % 2
                        mmv = 0
                        for sh in range(2):
                            mmv = K.emit("tensor", (lambda ee, s_: lambda g: g.matmul(
                                out=ps[2][:, s_ * 512:(s_ + 1) * 512],
                                lhsT=esel[0:E, ee * P:(ee + 1) * P],
                                rhs=wgT[0:E, s_ * 512:(s_ + 1) * 512], start=True, stop=True))(e, sh),
                                waits={"dve": max(ps_rel_dve[2], wgT_done)},
                                inc=("mm", 1) if sh == 1 else None)
                        d_wgb = K.emit("vector", (lambda q: lambda g: g.tensor_copy(
                            wgb[:, q * TC:(q + 1) * TC], ps[2][:, 0:TC]))(wsl),
                            waits={"mm": mmv}, inc=("dve", 1))
                        ps_rel_dve[2] = d_wgb
                        gelu_ms = {}
                        w1cast = 0
                        for jc in range(NJC):
                            if jc % 2 == 0:
                                pr = (jc // 2) % 2
                                wld = K.emit("sync", (lambda ee, j, q: lambda g: g.dma_start(
                                    out=w1st[:, q * 2048:(q + 1) * 2048].rearrange(
                                        "p (dt j) -> p dt j", j=2 * P),
                                    in_=w1_d[ee].rearrange("(dt p) h -> p dt h", p=P)[:, :, j * P:(j + 2) * P]))(e, jc, pr),
                                    waits={"dve": w1st_rel[pr]}, inc=(wdsem[pr], 16))
                                w1cast = K.emit("vector", (lambda q: lambda g: g.tensor_copy(
                                    w1b[:, q * 2048:(q + 1) * 2048], w1st[:, q * 2048:(q + 1) * 2048]))(pr),
                                    waits={wdsem[pr]: wld, "mm": w1b_rel[pr]}, inc=("dve", 1))
                                w1st_rel[pr] = w1cast
                            pr = (jc // 2) % 2
                            joff = jc % 2
                            hpr = jc % 2
                            mmv = 0
                            for sh in range(2):
                                for dt_i in range(NDT):
                                    w = {"dve": w1cast}
                                    if sh == 0 and dt_i == 0:
                                        w["act"] = gelu_ms.get(jc - 2, gelu_all_prev)
                                        w["dve"] = max(w1cast, ps_rel_dve[hpr])
                                    mmv = K.emit("tensor", (lambda q, jo, i, s_, hp: lambda g: g.matmul(
                                        out=ps[hp][:, s_ * 512:(s_ + 1) * 512],
                                        lhsT=w1b[:, q * 2048 + i * 256 + jo * P:
                                                 q * 2048 + i * 256 + jo * P + P],
                                        rhs=u1[:, i * TC + s_ * 512:i * TC + (s_ + 1) * 512],
                                        start=(i == 0), stop=(i == NDT - 1)))(pr, joff, dt_i, sh, hpr),
                                        waits=w,
                                        inc=("mm", 1) if (sh == 1 and dt_i == NDT - 1) else None)
                            w1b_rel[pr] = mmv
                            ps_rel_dve[hpr] = max(ps_rel_dve[hpr], 0)
                            gelu_ms[jc] = K.emit("scalar", (lambda ee, j, hp: lambda g: g.activation(
                                out=hsb[:, j * TC:(j + 1) * TC], in_=ps[hp][:, 0:TC],
                                func=AF.Gelu_apprx_tanh,
                                bias=b1all[:, ee * NJC + j:ee * NJC + j + 1]))(e, jc, hpr),
                                waits={"mm": mmv}, inc=("act", 1))
                        gelu_all = gelu_ms[NJC - 1]
                        gelu_all_prev = gelu_all
                        w2cast = 0
                        for half in range(2):
                            for jc in range(NJC):
                                if jc % 2 == 0:
                                    pr = (jc // 2) % 2
                                    wld = K.emit("sync", (lambda ee, j, hf, q: lambda g: g.dma_start(
                                        out=w2st[:, q * 1024:(q + 1) * 1024].rearrange(
                                            "p (jj o) -> p jj o", o=512),
                                        in_=w2_d[ee].rearrange("(jj p) o -> p jj o", p=P)[:, j:j + 2, hf * 512:(hf + 1) * 512]))(e, jc, half, pr),
                                        waits={"act": w2st_rel[pr]}, inc=(wesem[pr], 16))
                                    w2cast = K.emit("scalar", (lambda q: lambda g: g.activation(
                                        out=w2b[:, q * 1024:(q + 1) * 1024],
                                        in_=w2st[:, q * 1024:(q + 1) * 1024], func=AF.Copy))(pr),
                                        waits={wesem[pr]: wld, "mm": w2b_rel[pr]}, inc=("act", 1))
                                    w2st_rel[pr] = w2cast
                                pr = (jc // 2) % 2
                                joff = jc % 2
                                mmv = 0
                                for sh in range(2):
                                    for oc_rel in range(4):
                                        w = {"act": max(w2cast, gelu_ms[jc])}
                                        if jc == 0 and sh == 0:
                                            w["act"] = max(w["act"], gelu_all)
                                            w["dve"] = ps_rel_dve[oc_rel]
                                        mmv = K.emit("tensor", (lambda j, q, jo, s_, o: lambda g: g.matmul(
                                            out=ps[o][:, s_ * 512:(s_ + 1) * 512],
                                            lhsT=w2b[:, q * 1024 + jo * 512 + o * P:
                                                     q * 1024 + jo * 512 + o * P + P],
                                            rhs=hsb[:, j * TC + s_ * 512:j * TC + (s_ + 1) * 512],
                                            start=(j == 0), stop=(j == NJC - 1)))(jc, pr, joff, sh, oc_rel),
                                            waits=w,
                                            inc=("mm", 1) if (sh == 1 and oc_rel == 3) else None)
                                w2b_rel[pr] = mmv
                            m_p2 = mmv
                            for oc_rel in range(4):
                                oc = half * 4 + oc_rel
                                dvv = K.emit("vector", (lambda ee, o, orl, q: lambda g: g.scalar_tensor_tensor(
                                    out=tmpev[:, :], in0=ps[orl][:, 0:TC],
                                    scalar=b2all[:, ee * NDT + o:ee * NDT + o + 1],
                                    in1=wgb[:, q * TC:(q + 1) * TC],
                                    op0=ALU.add, op1=ALU.mult))(e, oc, oc_rel, wsl),
                                    waits={"mm": m_p2}, inc=("dve", 1))
                                ps_rel_dve[oc_rel] = dvv
                                K.emit("vector", (lambda o: lambda g: g.tensor_tensor(
                                    out=xc[:, o * TC:(o + 1) * TC], in0=xc[:, o * TC:(o + 1) * TC],
                                    in1=tmpev[:, :], op=ALU.add))(oc), inc=("dve", 1))
                    ev_end = K.c["dve"]
                    for dt_i in range(NDT):
                        store_ms = K.emit("scalar", (lambda i, tt: lambda g: g.dma_start(
                            out=dst_xf[i * P:(i + 1) * P, tt:tt + TC],
                            in_=xc[:, i * TC:(i + 1) * TC]))(dt_i, t0),
                            waits={"dve": ev_end}, inc=("xdma", 16))
            K.barrier()
            return store_ms

        ffn1_store = ffn_call(xf[0], xf[1], tin_store, "A")

        with ExitStack() as ms:
            def msb(name, shape, dt):
                return ms.enter_context(nc.sbuf_tensor(name, shape, dt))
            mx = msb("mx", [P, NDT * S], F32)
            xn = msb("xn", [P, NDT * S], BF16)
            xnsA = msb("xnsA", [P, NDT], F32)
            xnsB = msb("xnsB", [P, NDT], F32)
            xns = msb("xns", [P, NDT], F32)
            ps_rel_dve = [0, 0, 0, 0]
            ps_rel_act = [0, 0, 0, 0]

            xld = 0
            for dt_i in range(NDT):
                xld = K.emit("scalar", (lambda i: lambda g: g.dma_start(
                    out=mx[:, i * S:(i + 1) * S], in_=xf[1][i * P:(i + 1) * P, :]))(dt_i),
                    waits={"xdma": ffn1_store}, inc=("xdma", 16))
            ts2 = ExitStack()
            t_t = ts2.enter_context(nc.sbuf_tensor("t_t", [P, NDT * S], F32))
            mrowm = ts2.enter_context(nc.sbuf_tensor("mrowm", [1, S], F32))
            mrowr = ts2.enter_context(nc.sbuf_tensor("mrowr", [1, S], F32))
            mrows1 = ts2.enter_context(nc.sbuf_tensor("mrows1", [1, S], F32))
            bfscr2 = ts2.enter_context(nc.sbuf_tensor("bfscr2", [P, 2 * S], BF16))
            sq_ms = [0, 0]
            for dt_i in range(NDT):
                sl = dt_i % 2
                a1 = K.emit("scalar", (lambda i, q: lambda g: g.activation(
                    out=bfscr2[:, q * S:(q + 1) * S], in_=mx[:, i * S:(i + 1) * S],
                    func=AF.Copy))(dt_i, sl),
                    waits={"xdma": xld, "mm": sq_ms[sl]}, inc=("act", 1))
                mmv = 0
                for sc in range(4):
                    pj = 2 + sc // 2
                    mmv = K.emit("tensor", (lambda q, s_, j, i: lambda g: g.matmul(
                        out=ps[j][0:1, (s_ % 2) * 512:(s_ % 2 + 1) * 512], lhsT=onescb[:, :],
                        rhs=bfscr2[:, q * S + s_ * 512:q * S + (s_ + 1) * 512],
                        start=(i == 0), stop=(i == NDT - 1)))(sl, sc, pj, dt_i),
                        waits={"act": a1}, inc=("mm", 1) if sc == 3 else None)
                a2 = K.emit("scalar", (lambda i, q: lambda g: g.activation(
                    out=bfscr2[:, q * S:(q + 1) * S], in_=mx[:, i * S:(i + 1) * S],
                    func=AF.Square))(dt_i, sl),
                    waits={"mm": mmv}, inc=("act", 1))
                for sc in range(4):
                    pj = sc // 2
                    mmv = K.emit("tensor", (lambda q, s_, j, i: lambda g: g.matmul(
                        out=ps[j][0:1, (s_ % 2) * 512:(s_ % 2 + 1) * 512], lhsT=onescb[:, :],
                        rhs=bfscr2[:, q * S + s_ * 512:q * S + (s_ + 1) * 512],
                        start=(i == 0), stop=(i == NDT - 1)))(sl, sc, pj, dt_i),
                        waits={"act": a2}, inc=("mm", 1) if sc == 3 else None)
                sq_ms[sl] = mmv
            m_sums = K.c["mm"]
            dv = 0
            for hf in range(2):
                dv = K.emit("vector", (lambda h2: lambda g: g.tensor_scalar(
                    out=mrowm[0:1, h2 * 1024:(h2 + 1) * 1024], in0=ps[2 + h2][0:1, 0:1024],
                    scalar1=1.0 / D, scalar2=None, op0=ALU.mult))(hf),
                    waits={"mm": m_sums}, inc=("dve", 1))
            av = K.emit("scalar", lambda g: g.activation(out=mrows1[:, :], in_=mrowm[:, :],
                                                        func=AF.Square),
                        waits={"dve": dv}, inc=("act", 1))
            for hf in range(2):
                dv = K.emit("vector", (lambda h2: lambda g: g.scalar_tensor_tensor(
                    out=mrows1[0:1, h2 * 1024:(h2 + 1) * 1024], in0=ps[h2][0:1, 0:1024],
                    scalar=1.0 / D, in1=mrows1[0:1, h2 * 1024:(h2 + 1) * 1024],
                    op0=ALU.mult, op1=ALU.subtract))(hf),
                    waits={"act": av}, inc=("dve", 1))
            av = K.emit("scalar", lambda g: g.activation(out=mrowr[:, :], in_=mrows1[:, :],
                                                        func=AF.Sqrt, bias=1e-5),
                        waits={"dve": dv}, inc=("act", 1))
            d_rows = K.emit("vector", lambda g: g.reciprocal(mrows1[:, :], mrowr[:, :]),
                            waits={"act": av}, inc=("dve", 1))
            m_bc = 0
            for sc in range(4):
                K.emit("tensor", (lambda s_: lambda g: g.matmul(
                    out=ps[s_ // 2][:, (s_ % 2) * 512:(s_ % 2 + 1) * 512], lhsT=onesr[:, :],
                    rhs=mrowm[0:1, s_ * 512:(s_ + 1) * 512], start=True, stop=True))(sc),
                    waits={"dve": d_rows})
                m_bc = K.emit("tensor", (lambda s_: lambda g: g.matmul(
                    out=ps[2 + s_ // 2][:, (s_ % 2) * 512:(s_ % 2 + 1) * 512], lhsT=onesr[:, :],
                    rhs=mrows1[0:1, s_ * 512:(s_ + 1) * 512], start=True, stop=True))(sc),
                    waits={"dve": d_rows}, inc=("mm", 1))
            t_ms = 0
            for dt_i in range(NDT):
                for hf in range(2):
                    K.emit("vector", (lambda i, h2: lambda g: g.tensor_tensor(
                        out=t_t[:, i * S + h2 * 1024:i * S + (h2 + 1) * 1024],
                        in0=mx[:, i * S + h2 * 1024:i * S + (h2 + 1) * 1024],
                        in1=ps[h2][:, 0:1024], op=ALU.subtract))(dt_i, hf),
                        waits={"mm": m_bc}, inc=("dve", 1))
                    t_ms = K.emit("vector", (lambda i, h2: lambda g: g.tensor_tensor(
                        out=t_t[:, i * S + h2 * 1024:i * S + (h2 + 1) * 1024],
                        in0=t_t[:, i * S + h2 * 1024:i * S + (h2 + 1) * 1024],
                        in1=ps[2 + h2][:, 0:1024], op=ALU.mult))(dt_i, hf), inc=("dve", 1))
            sq_ms = [0, 0]
            mmv = 0
            for dt_i in range(NDT):
                sl = dt_i % 2
                a1 = K.emit("scalar", (lambda i, q: lambda g: g.activation(
                    out=bfscr2[:, q * S:(q + 1) * S], in_=t_t[:, i * S:(i + 1) * S],
                    func=AF.Square))(dt_i, sl),
                    waits={"dve": t_ms, "mm": sq_ms[sl]}, inc=("act", 1))
                for sc in range(4):
                    pj = sc // 2
                    w = {"act": a1}
                    if dt_i == 0 and sc == 0:
                        w["dve"] = t_ms
                    mmv = K.emit("tensor", (lambda q, s_, j, i: lambda g: g.matmul(
                        out=ps[j][0:1, (s_ % 2) * 512:(s_ % 2 + 1) * 512], lhsT=onescb[:, :],
                        rhs=bfscr2[:, q * S + s_ * 512:q * S + (s_ + 1) * 512],
                        start=(i == 0), stop=(i == NDT - 1)))(sl, sc, pj, dt_i),
                        waits=w, inc=("mm", 1) if sc == 3 else None)
                sq_ms[sl] = mmv
            m_tsq = K.c["mm"]
            dv = 0
            for hf in range(2):
                dv = K.emit("vector", (lambda h2: lambda g: g.tensor_scalar(
                    out=mrowr[0:1, h2 * 1024:(h2 + 1) * 1024], in0=ps[h2][0:1, 0:1024],
                    scalar1=1.0 / D, scalar2=None, op0=ALU.mult))(hf),
                    waits={"mm": m_tsq}, inc=("dve", 1))
            av = K.emit("scalar", lambda g: g.activation(out=mrowm[:, :], in_=mrowr[:, :],
                                                        func=AF.Sqrt, bias=1e-6),
                        waits={"dve": dv}, inc=("act", 1))
            dv = K.emit("vector", lambda g: g.reciprocal(mrowr[:, :], mrowm[:, :]),
                        waits={"act": av}, inc=("dve", 1))
            m_rb = 0
            for sc in range(4):
                m_rb = K.emit("tensor", (lambda s_: lambda g: g.matmul(
                    out=ps[s_ // 2][:, (s_ % 2) * 512:(s_ % 2 + 1) * 512], lhsT=onesr[:, :],
                    rhs=mrowr[0:1, s_ * 512:(s_ + 1) * 512], start=True, stop=True))(sc),
                    waits={"dve": dv}, inc=("mm", 1))
            xn_ms = 0
            for dt_i in range(NDT):
                for hf in range(2):
                    acc = xnsA if hf == 0 else xnsB
                    xn_ms = K.emit("vector", (lambda i, h2, a3: lambda g: g.scalar_tensor_tensor(
                        out=xn[:, i * S + h2 * 1024:i * S + (h2 + 1) * 1024],
                        in0=t_t[:, i * S + h2 * 1024:i * S + (h2 + 1) * 1024],
                        scalar=nw8[:, i:i + 1], in1=ps[h2][:, 0:1024],
                        op0=ALU.mult, op1=ALU.mult, accum_out=a3[:, i:i + 1]))(dt_i, hf, acc),
                        waits={"mm": m_rb}, inc=("dve", 1))
            d_xnsum = K.emit("vector", lambda g: g.tensor_tensor(
                out=xns[:, :], in0=xnsA[:, :], in1=xnsB[:, :], op=ALU.add), inc=("dve", 1))
            ps_rel_dve = [xn_ms, xn_ms, t_ms, t_ms]
            for dt_i in range(NDT):
                K.emit("vector", (lambda i: lambda g: g.tensor_tensor(
                    out=mx[:, i * S:(i + 1) * S], in0=mx[:, i * S:(i + 1) * S],
                    in1=t_t[:, i * S:(i + 1) * S], op=ALU.add))(dt_i), inc=("dve", 1))
            ts2.close()
            mmv = 0
            for dt_i in range(NDT):
                mmv = K.emit("tensor", (lambda i: lambda g: g.matmul(
                    out=ps[0][0:1, 0:8], lhsT=xns[:, i:i + 1], rhs=rwT[:, i * 8:(i + 1) * 8],
                    start=(i == 0), stop=(i == NDT - 1)))(dt_i),
                    waits={"dve": max(d_xnsum, ps_rel_dve[0])},
                    inc=("mm", 1) if dt_i == NDT - 1 else None)
            K.emit("vector", lambda g: g.tensor_scalar(
                out=r8a[:, :], in0=ps[0][0:1, 0:8], scalar1=1.0 / S, scalar2=None, op0=ALU.mult),
                waits={"mm": mmv}, inc=("dve", 1))
            K.emit("vector", lambda g: g.tensor_tensor(out=r8a[:, :], in0=r8a[:, :],
                                                       in1=rbrow[:, :], op=ALU.add), inc=("dve", 1))
            K.emit("vector", lambda g: g.reduce_max(r8b[0:1, 0:1], r8a[:, :], axis=AX),
                   inc=("dve", 1))
            dv = K.emit("vector", lambda g: g.tensor_scalar(
                out=r8b[0:1, 1:2], in0=r8b[0:1, 0:1], scalar1=-1.0, scalar2=None, op0=ALU.mult),
                inc=("dve", 1))
            av = K.emit("scalar", lambda g: g.activation(out=probrow[:, :], in_=r8a[:, :],
                                                        func=AF.Exp, bias=r8b[0:1, 1:2]),
                        waits={"dve": dv}, inc=("act", 1))
            K.emit("vector", lambda g: g.reduce_sum(r8b[0:1, 2:3], probrow[:, :], axis=AX),
                   waits={"act": av}, inc=("dve", 1))
            K.emit("vector", lambda g: g.reciprocal(r8b[0:1, 3:4], r8b[0:1, 2:3]), inc=("dve", 1))
            dv = K.emit("vector", lambda g: g.tensor_scalar(
                out=probrow[:, :], in0=probrow[:, :], scalar1=r8b[0:1, 3:4], scalar2=None,
                op0=ALU.mult), inc=("dve", 1))
            mmv = K.emit("tensor", lambda g: g.transpose(out=ps[1][0:8, 0:1], in_=probrow[:, :],
                                                        identity=ident[0:1, 0:1]),
                         waits={"dve": max(dv, ps_rel_dve[1])}, inc=("mm", 1))
            dvp = K.emit("vector", lambda g: g.tensor_copy(tiny[:, :], ps[1][0:8, 0:1]),
                         waits={"mm": mmv}, inc=("dve", 1))
            for e in range(E):
                mmv = K.emit("tensor", (lambda ee: lambda g: g.matmul(
                    out=ps[1][:, 8:9], lhsT=esel[0:E, ee * P:(ee + 1) * P], rhs=tiny[:, :],
                    start=True, stop=True))(e),
                    waits={"dve": K.c["dve"]}, inc=("mm", 1))
                K.emit("vector", (lambda ee: lambda g: g.tensor_copy(
                    pcols[:, ee:ee + 1], ps[1][:, 8:9]))(e),
                    waits={"mm": mmv}, inc=("dve", 1))
            d_pcols = K.c["dve"]

            with ExitStack() as hs2:
                hbf = hs2.enter_context(nc.sbuf_tensor("hbf", [P, NDT * S], BF16))
                hid_t = hs2.enter_context(nc.sbuf_tensor("hid_t", [P, S], F32))
                gat_t = hs2.enter_context(nc.sbuf_tensor("gat_t", [P, S], F32))
                ct_t = hs2.enter_context(nc.sbuf_tensor("ct_t", [P, S], F32))
                macc = hs2.enter_context(nc.sbuf_tensor("macc", [P, 2 * 1024], F32))
                wtT = hs2.enter_context(nc.sbuf_tensor("wtT", [P, NDT * 256], BF16))
                wotT = hs2.enter_context(nc.sbuf_tensor("wotT", [P, NDT * P], BF16))
                wst = hs2.enter_context(nc.sbuf_tensor("wst", [P, 5 * 1024], F32))
                ftmp = hs2.enter_context(nc.sbuf_tensor("ftmp", [P, 2 * 1024], F32))
                ftmp_rel = [0, 0]
                wst_rel = [("dve", d_pcols)] * 5
                wmsem = ["wm0", "wm1", "wm2", "wm3", "wm4"]
                slot_i = 0
                hg_done = 0
                for cq in range(NDT):
                    fma_h = 0; fma_g = 0
                    for e in range(E):
                        for part in range(2):
                            slq = slot_i % 5; slot_i += 1
                            roff = cq * P if part == 0 else D + cq * P
                            wld = K.emit("sync", (lambda ee, ro, q: lambda g: g.dma_start(
                                out=wst[:, q * 1024:(q + 1) * 1024],
                                in_=whg_d[ee, ro:ro + P, :]))(e, roff, slq),
                                waits={wst_rel[slq][0]: wst_rel[slq][1]}, inc=(wmsem[slq], 16))
                            tgt = 0 if part == 0 else 1024
                            if e == 0:
                                dvv = K.emit("vector", (lambda q, tg, ee: lambda g: g.tensor_scalar(
                                    out=macc[:, tg:tg + 1024], in0=wst[:, q * 1024:(q + 1) * 1024],
                                    scalar1=pcols[:, ee:ee + 1], scalar2=None, op0=ALU.mult))(slq, tgt, e),
                                    waits={wmsem[slq]: wld, "dve": d_pcols}, inc=("dve", 1))
                                wst_rel[slq] = ("dve", dvv)
                            else:
                                tsl = part
                                a_sc = K.emit("scalar", (lambda q, ee, t2: lambda g: g.activation(
                                    out=ftmp[:, t2 * 1024:(t2 + 1) * 1024],
                                    in_=wst[:, q * 1024:(q + 1) * 1024], func=AF.Copy,
                                    scale=pcols[:, ee:ee + 1]))(slq, e, tsl),
                                    waits={wmsem[slq]: wld, "dve": ftmp_rel[tsl]}, inc=("act", 1))
                                wst_rel[slq] = ("act", a_sc)
                                dvv = K.emit("vector", (lambda tg, t2: lambda g: g.tensor_tensor(
                                    out=macc[:, tg:tg + 1024], in0=macc[:, tg:tg + 1024],
                                    in1=ftmp[:, t2 * 1024:(t2 + 1) * 1024], op=ALU.add))(tgt, tsl),
                                    waits={"act": a_sc}, inc=("dve", 1))
                                ftmp_rel[tsl] = dvv
                            if part == 0:
                                fma_h = dvv
                            else:
                                fma_g = dvv
                    a_tp = 0
                    for part in range(2):
                        src_off = 0 if part == 0 else 1024
                        for dt_i in range(NDT):
                            pj = 2 + (dt_i % 2)
                            mmv = K.emit("tensor", (lambda so, i, j: lambda g: g.transpose(
                                out=ps[j][:, 128:256], in_=macc[:, so + i * P:so + (i + 1) * P],
                                identity=ident[:, :]))(src_off, dt_i, pj),
                                waits={"dve": max(fma_h, fma_g), "act": ps_rel_act[pj]},
                                inc=("mm", 1))
                            a_tp = K.emit("scalar", (lambda i, j, pt: lambda g: g.activation(
                                out=wtT[:, i * 256 + pt * P:i * 256 + (pt + 1) * P],
                                in_=ps[j][:, 128:256], func=AF.Copy))(dt_i, pj, part),
                                waits={"mm": mmv}, inc=("act", 1))
                            ps_rel_act[pj] = a_tp
                    mm_hid = 0; mm_gat = 0
                    for part in range(2):
                        for sc in range(4):
                            pj = (0 if part == 0 else 2) + sc // 2
                            mmv = 0
                            for dt_i in range(NDT):
                                w = {"act": max(a_tp, ps_rel_act[pj])} if dt_i == 0 else {"act": a_tp}
                                if dt_i == 0:
                                    w["dve"] = ps_rel_dve[pj]
                                mmv = K.emit("tensor", (lambda pt, s_, j, i: lambda g: g.matmul(
                                    out=ps[j][:, (s_ % 2) * 512:(s_ % 2 + 1) * 512],
                                    lhsT=wtT[:, i * 256 + pt * P:i * 256 + (pt + 1) * P],
                                    rhs=xn[:, i * S + s_ * 512:i * S + (s_ + 1) * 512],
                                    start=(i == 0), stop=(i == NDT - 1)))(part, sc, pj, dt_i),
                                    waits=w, inc=("mm", 1) if dt_i == NDT - 1 else None)
                            if part == 0:
                                mm_hid = mmv
                            else:
                                mm_gat = mmv
                    a_h = 0; a_g = 0
                    for part in range(2):
                        for hf in range(2):
                            pj = (0 if part == 0 else 2) + hf
                            dst = hid_t if part == 0 else gat_t
                            av2 = K.emit("scalar", (lambda j, d3, h2, pm: lambda g: g.activation(
                                out=d3[:, h2 * 1024:(h2 + 1) * 1024], in_=ps[j][:, 0:1024],
                                func=AF.Copy))(pj, dst, hf, part),
                                waits={"mm": (mm_hid if part == 0 else mm_gat)}, inc=("act", 1))
                            ps_rel_act[pj] = av2
                            if part == 0:
                                a_h = av2
                            else:
                                a_g = av2
                    a_c = K.emit("scalar", lambda g: g.activation(out=ct_t[:, :], in_=gat_t[:, :],
                                                                 func=AF.Sigmoid, scale=-1.0),
                                 waits={"act": a_g, "dve": hg_done}, inc=("act", 1))
                    a_z = 0
                    for hf2 in range(2):
                        a_z = K.emit("scalar", (lambda h3: lambda g: g.activation(
                            out=ps[h3][:, 0:1024], in_=gat_t[:, h3 * 1024:(h3 + 1) * 1024],
                            func=AF.Sigmoid))(hf2), inc=("act", 1))
                    a_s = K.emit("scalar", lambda g: g.activation(out=gat_t[:, :], in_=hid_t[:, :],
                                                                 func=AF.Sigmoid),
                                 waits={"act": a_h}, inc=("act", 1))
                    K.emit("vector", lambda g: g.tensor_scalar(out=hid_t[:, :], in0=hid_t[:, :],
                                                               scalar1=0.5, scalar2=None, op0=ALU.add),
                           waits={"act": a_s}, inc=("dve", 1))
                    K.emit("vector", lambda g: g.tensor_tensor(out=hid_t[:, :], in0=hid_t[:, :],
                                                               in1=gat_t[:, :], op=ALU.max),
                           inc=("dve", 1))
                    dvv2 = 0
                    for hf2 in range(2):
                        dvv2 = K.emit("vector", (lambda h3: lambda g: g.tensor_tensor(
                            out=hid_t[:, h3 * 1024:(h3 + 1) * 1024],
                            in0=hid_t[:, h3 * 1024:(h3 + 1) * 1024],
                            in1=ps[h3][:, 0:1024], op=ALU.mult))(hf2),
                            waits={"act": a_z}, inc=("dve", 1))
                    ps_rel_dve[0] = dvv2; ps_rel_dve[1] = dvv2
                    K.emit("vector", lambda g: g.tensor_tensor_scan(
                        out=gat_t[:, :], data0=ct_t[:, :], data1=hid_t[:, :], initial=0.0,
                        op0=ALU.mult, op1=ALU.add),
                        waits={"act": a_c}, inc=("dve", 1))
                    hg_done = K.emit("vector", (lambda c2: lambda g: g.tensor_copy(
                        hbf[:, c2 * S:(c2 + 1) * S], gat_t[:, :]))(cq), inc=("dve", 1))
                for oq in range(NDT):
                    fma_o = 0
                    for e in range(E):
                        slq = slot_i % 5; slot_i += 1
                        wld = K.emit("sync", (lambda ee, ro, q: lambda g: g.dma_start(
                            out=wst[:, q * 1024:(q + 1) * 1024],
                            in_=wout_d[ee, ro:ro + P, :]))(e, oq * P, slq),
                            waits={wst_rel[slq][0]: wst_rel[slq][1]}, inc=(wmsem[slq], 16))
                        if e == 0:
                            fma_o = K.emit("vector", (lambda q, ee: lambda g: g.tensor_scalar(
                                out=macc[:, 0:1024], in0=wst[:, q * 1024:(q + 1) * 1024],
                                scalar1=pcols[:, ee:ee + 1], scalar2=None, op0=ALU.mult))(slq, e),
                                waits={wmsem[slq]: wld}, inc=("dve", 1))
                            wst_rel[slq] = ("dve", fma_o)
                        else:
                            tsl = e % 2
                            a_sc = K.emit("scalar", (lambda q, ee, t2: lambda g: g.activation(
                                out=ftmp[:, t2 * 1024:(t2 + 1) * 1024],
                                in_=wst[:, q * 1024:(q + 1) * 1024], func=AF.Copy,
                                scale=pcols[:, ee:ee + 1]))(slq, e, tsl),
                                waits={wmsem[slq]: wld, "dve": ftmp_rel[tsl]}, inc=("act", 1))
                            wst_rel[slq] = ("act", a_sc)
                            fma_o = K.emit("vector", (lambda t2: lambda g: g.tensor_tensor(
                                out=macc[:, 0:1024], in0=macc[:, 0:1024],
                                in1=ftmp[:, t2 * 1024:(t2 + 1) * 1024], op=ALU.add))(tsl),
                                waits={"act": a_sc}, inc=("dve", 1))
                            ftmp_rel[tsl] = fma_o
                    a_tp = 0
                    for kt in range(NDT):
                        pj = 2 + (kt % 2)
                        mmv = K.emit("tensor", (lambda i, j: lambda g: g.transpose(
                            out=ps[j][:, 128:256], in_=macc[:, i * P:(i + 1) * P],
                            identity=ident[:, :]))(kt, pj),
                            waits={"dve": fma_o, "act": ps_rel_act[pj]}, inc=("mm", 1))
                        a_tp = K.emit("scalar", (lambda i, j: lambda g: g.activation(
                            out=wotT[:, i * P:(i + 1) * P], in_=ps[j][:, 128:256],
                            func=AF.Copy))(kt, pj),
                            waits={"mm": mmv}, inc=("act", 1))
                        ps_rel_act[pj] = a_tp
                    for hf in range(2):
                        pj = hf
                        mmv = 0
                        for sc in range(2):
                            for kt in range(NDT):
                                w = {"act": a_tp, "dve": hg_done}
                                if kt == 0 and sc == 0:
                                    w["dve"] = max(hg_done, ps_rel_dve[pj])
                                mmv = K.emit("tensor", (lambda h2, s_, j, i: lambda g: g.matmul(
                                    out=ps[j][:, s_ * 512:(s_ + 1) * 512],
                                    lhsT=wotT[:, i * P:(i + 1) * P],
                                    rhs=hbf[:, i * S + h2 * 1024 + s_ * 512:
                                            i * S + h2 * 1024 + (s_ + 1) * 512],
                                    start=(i == 0), stop=(i == NDT - 1)))(hf, sc, pj, kt),
                                    waits=w,
                                    inc=("mm", 1) if (kt == NDT - 1 and sc == 1) else None)
                        dvv = K.emit("vector", (lambda o, h2, j: lambda g: g.tensor_tensor(
                            out=mx[:, o * S + h2 * 1024:o * S + (h2 + 1) * 1024],
                            in0=mx[:, o * S + h2 * 1024:o * S + (h2 + 1) * 1024],
                            in1=ps[j][:, 0:1024], op=ALU.add))(oq, hf, pj),
                            waits={"mm": mmv}, inc=("dve", 1))
                        ps_rel_dve[pj] = dvv
            ev_end = K.c["dve"]
            mix_store = 0
            for dt_i in range(NDT):
                mix_store = K.emit("scalar", (lambda i: lambda g: g.dma_start(
                    out=xf[2][i * P:(i + 1) * P, :], in_=mx[:, i * S:(i + 1) * S]))(dt_i),
                    waits={"dve": ev_end}, inc=("xdma", 16))
        K.barrier()

        ffn2_store = ffn_call(xf[2], xf[3], mix_store, "B")

        with ExitStack() as tsc:
            xtm_o = tsc.enter_context(nc.sbuf_tensor("xtm2", [P, 16 * D], F32))
            ftile_o = tsc.enter_context(nc.sbuf_tensor("ftile2", [P, 16 * D], F32))
            for st in range(S // P):
                K.emit("scalar", (lambda s_: lambda g: g.dma_start(
                    out=xtm_o[:, s_ * D:(s_ + 1) * D].rearrange("p (dt s) -> p dt s", s=P),
                    in_=xf[3].rearrange("(dt p) s -> p dt s", p=P)[:, :, s_ * P:(s_ + 1) * P]))(st),
                    waits={"xdma": ffn2_store}, inc=("xdma", 16))
            ld_all = K.c["xdma"]
            cp_ms = [0, 0]
            for st in range(S // P):
                sl = st % 2
                mm_last = 0
                for dt_i in range(NDT):
                    mm_last = K.emit("tensor", (lambda s_, q, i: lambda g: g.transpose(
                        out=ps[q][:, i * P:(i + 1) * P],
                        in_=xtm_o[:, s_ * D + i * P:s_ * D + (i + 1) * P],
                        identity=ident[:, :]))(st, sl, dt_i),
                        waits={"xdma": ld_all, "dve": cp_ms[sl]}, inc=("mm", 1))
                cp_ms[sl] = K.emit("vector", (lambda s_, q: lambda g: g.tensor_copy(
                    ftile_o[:, s_ * D:(s_ + 1) * D], ps[q][:, :]))(st, sl),
                    waits={"mm": mm_last}, inc=("dve", 1))
            cp_all = K.c["dve"]
            for st in range(S // P):
                K.emit("scalar", (lambda s_: lambda g: g.dma_start(
                    out=out_d[s_ * P:(s_ + 1) * P, :], in_=ftile_o[:, s_ * D:(s_ + 1) * D]))(st),
                    waits={"dve": cp_all}, inc=("xdma", 16))
        K.emit("sync", lambda g: g.nop(), waits={"xdma": K.c["xdma"]})

        with nc.Block() as block:
            def replay(name):
                def _(eng):
                    waited = {}
                    for w, inc, fn in K.ops[name]:
                        for s2, v in w.items():
                            if v > waited.get(s2, 0):
                                eng.wait_ge(sems[s2], v)
                                waited[s2] = v
                        r = fn(eng)
                        if inc:
                            r.then_inc(sems[inc[0]], inc[1])
                return _
            block.tensor(replay("tensor"))
            block.vector(replay("vector"))
            block.scalar(replay("scalar"))
            block.gpsimd(replay("gpsimd"))
            block.sync(replay("sync"))

    return nc


_AUX = None


def _aux():
    global _AUX
    if _AUX is None:
        esel = np.zeros((E, E * P), dtype=np.float32)
        for e in range(E):
            esel[e, e * P:(e + 1) * P] = 1.0
        _AUX = {
            "ident": np.eye(P, dtype=np.float32),
            "esel": esel,
            "onesr": np.ones((1, P), dtype=np.float32),
            "onescf": np.ones((P, 1), dtype=np.float32),
            "cvals": np.array([[1e-5, 1e-6]], dtype=np.float32),
        }
    return _AUX


_NC_CACHE = None


def kernel(L0_attn_inputs, L0_attn_norm_weight, L0_attn_router_w, L0_attn_router_b,
           L0_attn_w_hg, L0_attn_w_out, L0_ffn_w_router, L0_ffn_w1, L0_ffn_b1,
           L0_ffn_w2, L0_ffn_b2):
    global _NC_CACHE
    if _NC_CACHE is None:
        _NC_CACHE = build_program(debug=False)
    nc = _NC_CACHE
    aux = _aux()
    f32 = lambda a: np.ascontiguousarray(a, dtype=np.float32)
    nw = f32(L0_attn_norm_weight)
    wr = f32(L0_ffn_w_router)
    b1 = f32(L0_ffn_b1)
    b2 = f32(L0_ffn_b2)
    common = {
        "rw": f32(L0_attn_router_w),
        "rb": f32(L0_attn_router_b).reshape(1, E),
        "whg": f32(L0_attn_w_hg),
        "wout": f32(L0_attn_w_out),
        "w1": f32(L0_ffn_w1),
        "w2": f32(L0_ffn_w2),
        "nw8h": np.ascontiguousarray(nw.reshape(NDT, P).T),
        "wr64h": np.ascontiguousarray(wr.reshape(NDT, P, E).transpose(1, 0, 2).reshape(P, NDT * E)),
        "b1ah": np.ascontiguousarray(b1.reshape(E, NJC, P).transpose(2, 0, 1).reshape(P, E * NJC)),
        "b2ah": np.ascontiguousarray(b2.reshape(E, NDT, P).transpose(2, 0, 1).reshape(P, E * NDT)),
        **aux,
    }
    x = f32(L0_attn_inputs)
    in_maps = [dict(common, inp=np.ascontiguousarray(x[b])) for b in range(8)]
    res = run_bass_kernel_spmd(nc, in_maps, core_ids=list(range(8)))
    return np.stack([r["out"] for r in res.results], axis=0)
